# version 12
# baseline (speedup 1.0000x reference)
import numpy as np
import ml_dtypes
import concourse.bass as bass
import concourse.mybir as mybir
from concourse.bass_utils import run_bass_kernel_spmd

F32 = mybir.dt.float32
BF16 = mybir.dt.bfloat16
FP8 = mybir.dt.float8e4
U8 = mybir.dt.uint8
AF = mybir.ActivationFunctionType
ALU = mybir.AluOpType

D = 2048
NCH = 16
HALO = 128
NTOK = 2176
NOUT = 2048
NI = 6144
NICH = 48
TBLK = [(0, 128), (128, 512), (640, 512), (1152, 512), (1664, 512)]
OBLK = [(128, 512), (640, 512), (1152, 512), (1664, 512)]
LN1E8 = 18.420680743952367
EPS = 1e-5

ENGS = ("sync", "gpsimd", "tensor", "vector", "scalar")
DMA_ENGS = ("sync", "gpsimd")


NDMASEM = 8


class B:

    def __init__(self):
        self.q = {e: [] for e in ENGS}
        self.cnt = {}
        self.owner = {}
        self.semnames = []
        for e in ENGS:
            if e in DMA_ENGS:
                for i in range(NDMASEM):
                    sn = f"{e}{i}"
                    self.cnt[sn] = 0
                    self.semnames.append(sn)
            else:
                self.cnt[e] = 0
                self.owner[e] = e
                self.semnames.append(e)
        self.rr = {e: 0 for e in DMA_ENGS}
        self.phase = "p0"

    def emit_sdma(self, fn, waits=()):
        ws = tuple(w for w in waits if w is not None)
        i = self.rr.get("sdma", 0)
        self.rr["sdma"] = (i + 1) % 4
        sn = f"sdma{i}"
        if sn not in self.cnt:
            self.cnt[sn] = 0
            self.semnames.append(sn)
        prev = self.cnt[sn]
        if prev:
            ws = ws + ((sn, prev),)
        self.cnt[sn] = prev + 16
        self.q["scalar"].append((ws, fn, (sn, 16), self.phase, True))
        return (sn, prev + 16)

    def emit(self, e, fn, waits=(), inc=None):
        ws = tuple(w for w in waits if w is not None)
        if e in DMA_ENGS:
            i = self.rr[e]
            self.rr[e] = (i + 1) % NDMASEM
            sn = f"{e}{i}"
            prev = self.cnt[sn]
            if prev:
                ws = ws + ((sn, prev),)
            self.cnt[sn] = prev + 16
            self.q[e].append((ws, fn, (sn, 16), self.phase, False))
            return (sn, prev + 16)
        if inc == 0:
            self.q[e].append((ws, fn, None, self.phase, False))
            return None
        self.cnt[e] += 1
        self.q[e].append((ws, fn, (e, 1), self.phase, False))
        return (e, self.cnt[e])

    def now(self):
        return tuple((sn, self.cnt[sn]) for sn in self.semnames if self.cnt[sn] > 0)

    def fence(self):
        toks = self.now()
        for e in ENGS:
            self.q[e].append((toks, None, None, self.phase, False))

    def realize(self, nc, block, sems):
        for e in ENGS:
            def make(ename):
                def body(eng):
                    seen = {sn: 0 for sn in self.semnames}
                    cur = None
                    scope = None
                    for ws, fn, inc, ph, force in self.q[ename]:
                        if ph != cur:
                            if scope is not None:
                                scope.__exit__(None, None, None)
                            scope = nc.named_scope(f"{ph}")
                            scope.__enter__()
                            cur = ph
                        for (we, wv) in ws:
                            if (not force and self.owner.get(we) == ename) or wv <= seen[we]:
                                continue
                            eng.wait_ge(sems[we], wv)
                            seen[we] = wv
                        if fn is None:
                            continue
                        inst = fn(eng)
                        if inc is not None:
                            inst.then_inc(sems[inc[0]], inc[1])
                    if scope is not None:
                        scope.__exit__(None, None, None)
                return body
            getattr(block, e)(make(e))


class Slot:

    def __init__(self):
        self.wtok = None
        self.users = []

    def write_waits(self):
        w = tuple(self.users) + ((self.wtok,) if self.wtok is not None else ())
        self.users = []
        return w

    def wrote(self, tok):
        if tok is not None:
            self.wtok = tok

    def rd(self):
        return self.wtok

    def used(self, tok):
        if tok is not None:
            self.users.append(tok)


class PsumPool:
    def __init__(self, tiles):
        self.tiles = tiles
        self.frees = [[] for _ in tiles]
        self.held = set()
        self.rr = 0

    def acquire(self):
        for _ in range(len(self.tiles)):
            i = self.rr
            self.rr = (self.rr + 1) % len(self.tiles)
            if i not in self.held:
                self.held.add(i)
                waits = tuple(self.frees[i])
                self.frees[i] = []
                return i, self.tiles[i], waits
        raise RuntimeError("all psum banks held")

    def release(self, i, tok):
        self.held.discard(i)
        if tok is not None:
            self.frees[i].append(tok)


def build_nc(debug=False, phases=("p1", "p2", "p3", "p4", "p6a", "p6b", "p5")):
    nc = bass.Bass("TRN2", target_bir_lowering=False, detect_race_conditions=False)

    def din(name, shape, dt=F32):
        return nc.dram_tensor(name, list(shape), dt, kind="ExternalInput").ap()

    xT = din("xT", (D, NTOK))
    wTn = ("wrT", "wkT", "wvT", "woT", "cwrT", "cwkT", "cwvT")
    wT = {n: din(n, (D, D)) for n in wTn}
    gw = din("gw", (D, NI))
    uw = din("uw", (D, NI))
    dw = din("dw", (NI, D))
    v16names = ("g1", "b1", "g2", "b2", "mur", "muk", "muv", "uu", "cmur", "cmuk", "ds")
    v16 = {n: din("i_" + n, (128, NCH)) for n in v16names}
    gs_i = din("i_gs", (128, NICH))
    us_i = din("i_us", (128, NICH))
    mask_i = din("i_mask", (128, 8))
    onescol_i = din("i_onescol", (128, 1), BF16)
    onesrow_i = din("i_onesrow", (1, 128))
    eps_i = din("i_eps", (128, 1))
    out = nc.dram_tensor("out", [D, NOUT], F32, kind="ExternalOutput").ap()

    skind = "ExternalOutput" if debug else "Internal"

    def scr(name, shape, dt):
        return nc.dram_tensor(name, list(shape), dt, kind=skind).ap()

    s_xn32 = scr("s_xn32", (NCH, 128, NTOK), F32)
    s_xnbf = scr("s_xnbf", (NCH, 128, NTOK), BF16)
    s_xr = scr("s_xr", (NCH, 128, NTOK), BF16)
    s_xk = scr("s_xk", (NCH, 128, NTOK), BF16)
    s_xv = scr("s_xv", (NCH, 128, NTOK), BF16)
    s_a = scr("s_a", (NCH, 128, NTOK), BF16)
    s_x2 = scr("s_x2", (NCH, 128, NTOK), F32)
    s_xr2 = scr("s_xr2", (NCH, 128, NTOK), BF16)
    s_xk2 = scr("s_xk2", (NCH, 128, NTOK), BF16)
    s_r2 = scr("s_r2", (NCH, 128, NTOK), BF16)
    s_k2 = scr("s_k2", (NCH, 128, NTOK), BF16)
    s_fuse = scr("s_fuse", (NICH, 128, NOUT), BF16)
    s_dwq = scr("s_dwq", (NICH, 128, D), BF16)
    s_ffn = scr("s_ffn", (NCH, 128, NOUT), F32)
    s_t1 = scr("s_t1", (NCH, 128, NOUT), F32) if debug else None

    def cm3(ap):
        return ap.rearrange("a p t -> p a t")

    xT3 = xT.rearrange("(a p) t -> p a t", p=128)
    out3 = out.rearrange("(a p) t -> p a t", p=128)

    import contextlib as _ctx
    _ph_stack = [None]

    def phase_open():
        if _ph_stack[0] is not None:
            _ph_stack[0].close()
        _ph_stack[0] = _ctx.ExitStack()

    def A(name, dt, cols, parts=128):
        return _ph_stack[0].enter_context(
            nc.sbuf_tensor(name, [parts, cols], dt)).ap()

    def A3(name, dt, nch, cols):
        return A(name, dt, nch * cols).rearrange("p (a t) -> p a t", a=nch)

    def A8(name, cols):
        u = _ph_stack[0].enter_context(nc.sbuf_tensor(name, [128, cols], U8)).ap()
        return u.bitcast(FP8)

    def A83(name, nch, cols):
        return A8(name, nch * cols).rearrange("p (a t) -> p a t", a=nch)

    cons = {}
    for n in v16names + ("murm1", "mukm1", "muvm1", "cmurm1", "cmukm1", "uprime"):
        cons[n] = nc.alloc_sbuf_tensor("c_" + n, [128, NCH], F32).ap()
    c_gs = nc.alloc_sbuf_tensor("c_gs", [128, NICH], F32).ap()
    c_us = nc.alloc_sbuf_tensor("c_us", [128, NICH], F32).ap()
    c_mask = nc.alloc_sbuf_tensor("c_mask", [128, 8], F32).ap()
    c_ones_bf = nc.alloc_sbuf_tensor("c_ones_bf", [128, 1], BF16).ap()
    c_onesrow = nc.alloc_sbuf_tensor("c_onesrow", [1, 128], F32).ap()
    c_eps = nc.alloc_sbuf_tensor("c_eps", [128, 1], F32).ap()
    c_carry = nc.alloc_sbuf_tensor("c_carry", [128, NCH], F32).ap()
    c_carry2 = nc.alloc_sbuf_tensor("c_carry2", [128, NCH], F32).ap()
    c_tcol = nc.alloc_sbuf_tensor("c_tcol", [128, NCH], F32).ap()

    EPS_AP = c_eps[:, 0:1]
    psum_t = [nc.alloc_psum_tensor(f"psb{i}", [128, 512], F32).ap() for i in range(8)]

    b = B()
    pool = PsumPool(psum_t)

    ld = []
    for n in v16names:
        ld.append(b.emit("sync", lambda e, d=cons[n], s=v16[n]: e.dma_start(d, s)))
    for d_, s_ in ((c_gs, gs_i), (c_us, us_i), (c_mask, mask_i), (c_ones_bf, onescol_i),
                   (c_onesrow, onesrow_i), (c_eps, eps_i)):
        ld.append(b.emit("sync", lambda e, d=d_, s=s_: e.dma_start(d, s)))
    allld = tuple(ld)
    for srcn, dstn in (("mur", "murm1"), ("muk", "mukm1"), ("muv", "muvm1"),
                       ("cmur", "cmurm1"), ("cmuk", "cmukm1")):
        b.emit("vector", lambda e, s=cons[srcn], d=cons[dstn]:
               e.tensor_scalar(d, s, -1.0, None, ALU.add), waits=allld)
    b.emit("vector", lambda e: e.tensor_scalar(cons["uprime"], cons["uu"], LN1E8, None, ALU.add),
           waits=allld)
    b.emit("vector", lambda e: e.memset(c_carry, 0.0))
    b.emit("vector", lambda e: e.memset(c_carry2, 0.0))
    b.emit("vector", lambda e: e.memset(c_tcol, 0.0))
    b.fence()

    def ln_from_rows(Tb, rowA, rowB, lnm, lnr, lnt, slA, slB, sl_lnt, extra=()):
        iS, pS, wS = pool.acquire()
        iQ, pQ, wQ = pool.acquire()
        tS = b.emit("tensor", lambda e, p=pS: e.matmul(p[:, 0:Tb], c_onesrow, rowA[0:1, 0:Tb],
                                                       start=True, stop=True),
                    waits=wS + extra)
        slA.used(tS)
        tQ = b.emit("tensor", lambda e, p=pQ: e.matmul(p[:, 0:Tb], c_onesrow, rowB[0:1, 0:Tb],
                                                       start=True, stop=True),
                    waits=wQ + extra)
        slB.used(tQ)
        tm = b.emit("vector", lambda e, p=pS: e.tensor_scalar(lnm[:, 0:Tb], p[:, 0:Tb],
                                                              1.0 / D, None, ALU.mult),
                    waits=(tS,))
        tq = b.emit("vector", lambda e, p=pQ: e.tensor_scalar(lnt[:, 0:Tb], p[:, 0:Tb],
                                                              1.0 / D, None, ALU.mult),
                    waits=(tQ,) + sl_lnt.write_waits())
        pool.release(iS, tm)
        pool.release(iQ, tq)
        b.emit("vector", lambda e: e.tensor_tensor(lnr[:, 0:Tb], lnm[:, 0:Tb], lnm[:, 0:Tb],
                                                   ALU.mult))
        t2 = b.emit("vector", lambda e: e.tensor_tensor(lnt[:, 0:Tb], lnt[:, 0:Tb], lnr[:, 0:Tb],
                                                        ALU.subtract))
        t3 = b.emit("scalar", lambda e: e.activation(lnt[:, 0:Tb], lnt[:, 0:Tb], AF.Sqrt,
                                                     bias=EPS_AP, scale=1.0), waits=(t2,))
        sl_lnt.wrote(t3)
        t4 = b.emit("vector", lambda e: e.reciprocal(lnr[:, 0:Tb], lnt[:, 0:Tb]), waits=(t3,))
        sl_lnt.used(t4)
        return tm, t4

    def colsum_pair(Tb, vbf, vsq, rdy, rowA, rowB, slA, slB):
        toks = []
        for vv, row, sl in ((vbf, rowA, slA), (vsq, rowB, slB)):
            ii, pp, wf = pool.acquire()
            tok = None
            for j in range(NCH):
                tok = b.emit("tensor", lambda e, p=pp, v=vv, j=j, Tb=Tb: e.matmul(
                    p[0:1, 0:Tb], c_ones_bf, v[:, j, 0:Tb],
                    start=(j == 0), stop=(j == NCH - 1)),
                    waits=(wf + rdy) if j == 0 else (), inc=(1 if j == NCH - 1 else 0))
            tc = b.emit("vector", lambda e, p=pp, r=row, Tb=Tb: e.tensor_copy(
                r[0:1, 0:Tb], p[0:1, 0:Tb]), waits=(tok,) + sl.write_waits())
            sl.wrote(tc)
            pool.release(ii, tc)
            toks.append(tc)
        return tuple(toks)

    def p1():
        phase_open()
        x3 = A3("p1x", F32, NCH, 512)
        xn3 = A3("p1xn", F32, NCH, 512)
        dd3 = A3("p1d", F32, NCH, 512)
        h3 = A3("p1h", F32, NCH, 512)
        xbf3 = A3("p1xbf", BF16, NCH, 512)
        xsq3 = A3("p1xsq", BF16, NCH, 512)
        stg = {n: [A3(f"p1stg_{n}{h}", BF16, 4, 512) for h in range(2)]
               for n in ("xr", "xk", "xv")}
        lnm = A("p1lnm", F32, 512)
        lnr = A("p1lnr", F32, 512)
        lnt = A("p1lnt", F32, 512)
        rowAv = A("p1_rowA", F32, 512, parts=1)
        rowBv = A("p1_rowB", F32, 512, parts=1)
        sl_rowA = Slot(); sl_rowB = Slot(); sl_lnt = Slot()

        sl_x = Slot(); sl_xn = Slot(); sl_hp = Slot(); sl_h = Slot()
        sl_xbf = Slot(); sl_xsq = Slot()
        sl_stg = {n: [Slot(), Slot()] for n in stg}
        dsts = {"xr": s_xr, "xk": s_xk, "xv": s_xv}
        mu1 = {"xr": "murm1", "xk": "mukm1", "xv": "muvm1"}

        for bi, (off, Tb) in enumerate(TBLK):
            txl = b.emit("sync", lambda e, off=off, Tb=Tb: e.dma_start(
                x3[:, :, 0:Tb], xT3[:, :, off:off + Tb]), waits=sl_x.write_waits())
            sl_x.wrote(txl)
            txb = b.emit("scalar", lambda e, Tb=Tb: e.copy(xbf3[:, :, 0:Tb], x3[:, :, 0:Tb]),
                         waits=(txl,) + sl_xbf.write_waits())
            sl_xbf.wrote(txb); sl_x.used(txb)
            txq = b.emit("scalar", lambda e, Tb=Tb: e.activation(xsq3[:, :, 0:Tb], x3[:, :, 0:Tb],
                                                                 AF.Square),
                         waits=(txl,) + sl_xsq.write_waits())
            sl_xsq.wrote(txq); sl_x.used(txq)
            tA, tB = colsum_pair(Tb, xbf3, xsq3, (txb, txq), rowAv, rowBv, sl_rowA, sl_rowB)
            sl_xbf.used(tA); sl_xsq.used(tB)
            tmm, trs = ln_from_rows(Tb, rowAv, rowBv, lnm, lnr, lnt,
                                    sl_rowA, sl_rowB, sl_lnt, extra=(tA, tB))
            mb = lnm[:, 0:Tb].unsqueeze(1).broadcast_to((128, NCH, Tb))
            rb = lnr[:, 0:Tb].unsqueeze(1).broadcast_to((128, NCH, Tb))
            t5 = b.emit("vector", lambda e, Tb=Tb, mb=mb: e.tensor_tensor(
                xn3[:, :, 0:Tb], x3[:, :, 0:Tb], mb, ALU.subtract),
                waits=(tmm, trs, txl) + sl_xn.write_waits())
            sl_x.used(t5)
            t6 = b.emit("vector", lambda e, Tb=Tb, rb=rb: e.tensor_tensor(
                xn3[:, :, 0:Tb], xn3[:, :, 0:Tb], rb, ALU.mult))
            sl_xn.wrote(t6)
            sp1 = b.emit("gpsimd", lambda e, off=off, Tb=Tb: e.dma_start(
                cm3(s_xn32)[:, :, off:off + Tb], xn3[:, :, 0:Tb]), waits=(t6,))
            sl_xn.used(sp1)
            txnb = b.emit("scalar", lambda e, Tb=Tb: e.copy(xbf3[:, :, 0:Tb], xn3[:, :, 0:Tb]),
                          waits=(t6,) + sl_xbf.write_waits())
            sl_xbf.wrote(txnb); sl_xn.used(txnb)
            sp2 = b.emit("gpsimd", lambda e, off=off, Tb=Tb: e.dma_start(
                cm3(s_xnbf)[:, :, off:off + Tb], xbf3[:, :, 0:Tb]), waits=(txnb,))
            sl_xbf.used(sp2)
            txq2 = b.emit("scalar", lambda e, Tb=Tb: e.activation(
                xsq3[:, :, 0:Tb], xn3[:, :, 0:Tb], AF.Square), waits=(t6,) + sl_xsq.write_waits())
            sl_xsq.wrote(txq2); sl_xn.used(txq2)
            tA2, tB2 = colsum_pair(Tb, xbf3, xsq3, (txnb, txq2), rowAv, rowBv, sl_rowA, sl_rowB)
            sl_xbf.used(tA2); sl_xsq.used(tB2)
            tmm2, trs2 = ln_from_rows(Tb, rowAv, rowBv, lnm, lnr, lnt,
                                      sl_rowA, sl_rowB, sl_lnt, extra=(tA2, tB2))
            mb2 = lnm[:, 0:Tb].unsqueeze(1).broadcast_to((128, NCH, Tb))
            rb2 = lnr[:, 0:Tb].unsqueeze(1).broadcast_to((128, NCH, Tb))
            hw = sl_h.write_waits()
            t7 = b.emit("vector", lambda e, Tb=Tb, mb2=mb2: e.tensor_tensor(
                h3[:, :, 0:Tb], xn3[:, :, 0:Tb], mb2, ALU.subtract),
                waits=(tmm2, trs2) + hw)
            sl_xn.used(t7)
            b.emit("vector", lambda e, Tb=Tb, rb2=rb2: e.tensor_tensor(
                h3[:, :, 0:Tb], h3[:, :, 0:Tb], rb2, ALU.mult))
            th = None
            for j in range(NCH):
                th = b.emit("vector", lambda e, j=j, Tb=Tb: e.tensor_scalar(
                    h3[:, j, 0:Tb], h3[:, j, 0:Tb], cons["g1"][:, j:j + 1],
                    cons["b1"][:, j:j + 1], ALU.mult, ALU.add))
            sl_h.wrote(th)
            dw_ = sl_hp.write_waits()
            td1 = b.emit("vector", lambda e, Tb=Tb: e.tensor_tensor(
                dd3[:, :, 1:Tb], h3[:, :, 1:Tb], h3[:, :, 0:Tb - 1], ALU.subtract), waits=dw_)
            dcol = dd3[:, :, 0:1].rearrange("p a t -> p (a t)")
            hcol = h3[:, :, 0:1].rearrange("p a t -> p (a t)")
            td2 = b.emit("vector", lambda e: e.tensor_tensor(dcol, hcol, c_tcol, ALU.subtract))
            sl_hp.wrote(td2)
            if bi + 1 < len(TBLK):
                b.emit("vector", lambda e, Tb=Tb, bi=bi: e.tensor_scalar(
                    c_tcol, h3[:, :, Tb - 1:Tb].rearrange("p a t -> p (a t)"),
                    c_mask[:, bi + 1:bi + 2], None, ALU.mult))
            for n in ("xr", "xk", "xv"):
                for g in range(4):
                    half = (bi * 4 + g) % 2
                    stile = stg[n][half]
                    sslot = sl_stg[n][half]
                    wfree = sslot.write_waits()
                    tl = None
                    for jj in range(4):
                        j = g * 4 + jj
                        tl = b.emit("vector", lambda e, j=j, jj=jj, Tb=Tb, st=stile, m=mu1[n]:
                                    e.scalar_tensor_tensor(
                                        st[:, jj, 0:Tb], dd3[:, j, 0:Tb],
                                        cons[m][:, j:j + 1], h3[:, j, 0:Tb],
                                        ALU.mult, ALU.add),
                                    waits=((td1, td2) + wfree) if jj == 0 else ())
                    sslot.wrote(tl)
                    sl_hp.used(tl); sl_h.used(tl)
                    tsp = b.emit("gpsimd", lambda e, off=off, Tb=Tb, st=stile, sd=dsts[n], g=g:
                                 e.dma_start(cm3(sd)[:, g * 4:(g + 1) * 4, off:off + Tb],
                                             st[:, :, 0:Tb]), waits=(tl,))
                    sslot.used(tsp)

    def p2():
        phase_open()
        wv = {n: A3(f"p2w_{n}", BF16, NCH, 512)
              for n in ("wrT", "wkT", "wvT")}
        actv = {n: [A3(f"p2a_{n}{h}", BF16, NCH, 512) for h in range(2)]
                for n in ("xr", "xk", "xv")}
        astg = [A3(f"p2astg{h}", BF16, 4, 512) for h in range(2)]
        rt = [A(f"p2rt{h}", F32, 512) for h in range(2)]
        ft = [A(f"p2ft{h}", F32, 512) for h in range(2)]
        srcs = {"xr": s_xr, "xk": s_xk, "xv": s_xv}
        wname = {"xr": "wrT", "xk": "wkT", "xv": "wvT"}

        sl_w = {n: [Slot(), Slot()] for n in wv}
        sl_act = {n: [Slot(), Slot()] for n in actv}
        sl_ast = [Slot(), Slot()]
        sl_r = [Slot(), Slot()]
        sl_f = [Slot(), Slot()]

        for wb in range(4):
            wtoks = {0: [], 1: []}
            for hf in range(2):
                for n in ("xr", "xk", "xv"):
                    wn = wname[n]
                    t = b.emit("gpsimd", lambda e, wn=wn, wb=wb, hf=hf: e.dma_start(
                        wv[wn][:, :, hf * 256:(hf + 1) * 256],
                        wT[wn].rearrange("(a p) d -> p a d", p=128)[
                            :, :, wb * 512 + hf * 256:wb * 512 + (hf + 1) * 256]),
                        waits=sl_w[wn][hf].write_waits())
                    sl_w[wn][hf].wrote(t)
                    wtoks[hf].append(t)
            wtoks = {k: tuple(v) for k, v in wtoks.items()}
            for ti, (off, Tb) in enumerate(TBLK):
                half = ti % 2
                atoks = []
                for n in ("xr", "xk", "xv"):
                    sl = sl_act[n][half]
                    t = b.emit("sync", lambda e, n=n, off=off, Tb=Tb, half=half: e.dma_start(
                        actv[n][half][:, :, 0:Tb], cm3(srcs[n])[:, :, off:off + Tb]),
                        waits=sl.write_waits())
                    sl.wrote(t)
                    atoks.append(t)
                atoks = tuple(atoks)
                for dj in range(4):
                    d_glob = wb * 4 + dj
                    hf = dj // 2
                    ptoks = {}
                    banks = {}
                    for n in ("xr", "xk", "xv"):
                        ii, pp, wfree = pool.acquire()
                        banks[n] = (ii, pp)
                        tok = None
                        for j in range(NCH):
                            tok = b.emit("tensor", lambda e, n=n, j=j, dj=dj, pp=pp, half=half,
                                         Tb=Tb: e.matmul(
                                             pp[:, 0:Tb],
                                             wv[wname[n]][:, j, dj * 128:(dj + 1) * 128],
                                             actv[n][half][:, j, 0:Tb],
                                             start=(j == 0), stop=(j == NCH - 1)),
                                         waits=(wfree + atoks + wtoks[hf]) if j == 0 else (),
                                         inc=(1 if j == NCH - 1 else 0))
                        ptoks[n] = tok
                        sl_w[wname[n]][hf].used(tok)
                        sl_act[n][half].used(tok)
                    ah = (ti * 4 + dj) % 2
                    tr = b.emit("scalar", lambda e, pp=banks["xr"][1], Tb=Tb, ah=ah:
                                e.activation(rt[ah][:, 0:Tb], pp[:, 0:Tb], AF.Sigmoid),
                                waits=(ptoks["xr"],) + sl_r[ah].write_waits())
                    sl_r[ah].wrote(tr)
                    tf = b.emit("scalar", lambda e, pp=banks["xk"][1], Tb=Tb, ah=ah, dg=d_glob:
                                e.activation(ft[ah][:, 0:Tb], pp[:, 0:Tb], AF.Sigmoid,
                                             bias=cons["uprime"][:, dg:dg + 1]),
                                waits=(ptoks["xk"],) + sl_f[ah].write_waits())
                    sl_f[ah].wrote(tf)
                    pool.release(banks["xr"][0], tr)
                    pool.release(banks["xk"][0], tf)
                    tv1 = b.emit("vector", lambda e, pp=banks["xv"][1], Tb=Tb, ah=ah:
                                 e.tensor_tensor(ft[ah][:, 0:Tb], ft[ah][:, 0:Tb], pp[:, 0:Tb],
                                                 ALU.mult), waits=(ptoks["xv"], tf))
                    sl_f[ah].wrote(tv1)
                    pool.release(banks["xv"][0], tv1)
                    ast = sl_ast[half]
                    wfree = ast.write_waits() if dj == 0 else ()
                    ta = b.emit("vector", lambda e, Tb=Tb, ah=ah, half=half, dj=dj:
                                e.tensor_tensor(astg[half][:, dj, 0:Tb], rt[ah][:, 0:Tb],
                                                ft[ah][:, 0:Tb], ALU.mult),
                                waits=(tv1, tr) + wfree)
                    ast.wrote(ta)
                    sl_r[ah].used(ta)
                    sl_f[ah].used(ta)
                    if dj == 3:
                        tsp = b.emit_sdma(lambda e, off=off, Tb=Tb, wb=wb, half=half:
                                          e.dma_start(cm3(s_a)[:, wb * 4:(wb + 1) * 4, off:off + Tb],
                                                      astg[half][:, :, 0:Tb]), waits=(ta,))
                        ast.used(tsp)

    def p3():
        phase_open()
        wvt = A3("p3wvt", BF16, NCH, 2048)
        av = A3("p3av", BF16, NCH, 512)
        _xn = A3("p3xn", F32, 8, 512)
        xnv = [_xn, _xn]
        x2f = A3("p3x2", F32, NCH, 512)
        dd3 = A3("p3d", BF16, NCH, 512)
        stg = {n: [A3(f"p3stg_{n}{h}", BF16, 4, 256) for h in range(2)]
               for n in ("xr2", "xk2")}
        lnm = A("p3lnm", F32, 512)
        lnr = A("p3lnr", F32, 512)
        lnt = A("p3lnt", F32, 512)
        x2bf = A3("p3x2bf", BF16, NCH, 512)
        x2sq = A3("p3x2sq", BF16, NCH, 512)
        rowAv = A("p3_rowA", F32, 512, parts=1)
        rowBv = A("p3_rowB", F32, 512, parts=1)
        sl_rowA = Slot(); sl_rowB = Slot(); sl_lnt = Slot()
        sl_w = [Slot(), Slot()]
        sl_a = Slot()
        _slxn = Slot()
        sl_xn = [_slxn, _slxn]
        sl_x2 = Slot()
        sl_d2 = Slot()
        sl_bf = Slot()
        sl_sq = Slot()
        sl_stg = {n: [Slot(), Slot()] for n in stg}
        dsts = {"xr2": s_xr2, "xk2": s_xk2}
        mu1 = {"xr2": "cmurm1", "xk2": "cmukm1"}

        twh = []
        for hf in range(2):
            t = b.emit("gpsimd", lambda e, hf=hf: e.dma_start(
                wvt[:, :, hf * 1024:(hf + 1) * 1024],
                wT["woT"].rearrange("(a p) d -> p a d", p=128)[
                    :, :, hf * 1024:(hf + 1) * 1024]),
                waits=sl_w[hf].write_waits())
            sl_w[hf].wrote(t)
            twh.append(t)

        for bi, (off, Tb) in enumerate(TBLK):
            ta = b.emit("sync", lambda e, off=off, Tb=Tb: e.dma_start(
                av[:, :, 0:Tb], cm3(s_a)[:, :, off:off + Tb]), waits=sl_a.write_waits())
            sl_a.wrote(ta)
            x2w = sl_x2.write_waits()
            casts = []
            for wh in range(2):
                xh = (bi * 2 + wh) % 2
                tx = b.emit("sync", lambda e, off=off, Tb=Tb, xh=xh, wh=wh: e.dma_start(
                    xnv[xh][:, :, 0:Tb], cm3(s_xn32)[:, wh * 8:(wh + 1) * 8, off:off + Tb]),
                    waits=sl_xn[xh].write_waits())
                sl_xn[xh].wrote(tx)
                for dj in range(8):
                    d_glob = wh * 8 + dj
                    hf = d_glob // 8
                    ii, pp, wfree = pool.acquire()
                    tok = None
                    for j in range(NCH):
                        tok = b.emit("tensor", lambda e, j=j, dg=d_glob, pp=pp, Tb=Tb:
                                     e.matmul(pp[:, 0:Tb], wvt[:, j, dg * 128:(dg + 1) * 128],
                                              av[:, j, 0:Tb],
                                              start=(j == 0), stop=(j == NCH - 1),
                                              skip_group_check=True),
                                     waits=(wfree + (ta, tx, twh[hf])) if j == 0 else (),
                                     inc=(1 if j == NCH - 1 else 0))
                    sl_w[hf].used(tok); sl_a.used(tok)
                    t1 = b.emit("vector", lambda e, pp=pp, Tb=Tb, xh=xh, dj=dj, dg=d_glob:
                                e.tensor_tensor(x2f[:, dg, 0:Tb], pp[:, 0:Tb],
                                                xnv[xh][:, dj, 0:Tb], ALU.add),
                                waits=(tok,) + (x2w if d_glob == 0 else ()))
                    sl_x2.wrote(t1)
                    sl_xn[xh].used(t1)
                    pool.release(ii, t1)
                    tsp = b.emit("gpsimd", lambda e, off=off, Tb=Tb, dg=d_glob: e.dma_start(
                        cm3(s_x2)[:, dg, off:off + Tb], x2f[:, dg, 0:Tb]), waits=(t1,))
                    sl_x2.used(tsp)
                t2 = b.emit("scalar", lambda e, Tb=Tb, wh=wh: e.copy(
                    x2bf[:, wh * 8:(wh + 1) * 8, 0:Tb], x2f[:, wh * 8:(wh + 1) * 8, 0:Tb]),
                    waits=(t1,) + (sl_bf.write_waits() if wh == 0 else ()))
                sl_bf.wrote(t2); sl_x2.used(t2)
                t3 = b.emit("scalar", lambda e, Tb=Tb, wh=wh: e.activation(
                    x2sq[:, wh * 8:(wh + 1) * 8, 0:Tb], x2f[:, wh * 8:(wh + 1) * 8, 0:Tb],
                    AF.Square), waits=(sl_sq.write_waits() if wh == 0 else ()))
                sl_sq.wrote(t3); sl_x2.used(t3)
                casts.append((t2, t3))
            iA, pA, wfA = pool.acquire()
            iB, pB, wfB = pool.acquire()
            tokA = tokB = None
            cA = (casts[0][0], casts[1][0])
            cB = (casts[0][1], casts[1][1])
            for dg in range(NCH):
                tokA = b.emit("tensor", lambda e, pA=pA, Tb=Tb, dg=dg: e.matmul(
                    pA[0:1, 0:Tb], c_ones_bf, x2bf[:, dg, 0:Tb],
                    start=(dg == 0), stop=(dg == 15)),
                    waits=(wfA + cA) if dg == 0 else (), inc=(1 if dg == 15 else 0))
            for dg in range(NCH):
                tokB = b.emit("tensor", lambda e, pB=pB, Tb=Tb, dg=dg: e.matmul(
                    pB[0:1, 0:Tb], c_ones_bf, x2sq[:, dg, 0:Tb],
                    start=(dg == 0), stop=(dg == 15)),
                    waits=(wfB + cB) if dg == 0 else (), inc=(1 if dg == 15 else 0))
            sl_bf.used(tokA)
            sl_sq.used(tokB)
            tA = b.emit("vector", lambda e, pA=pA, Tb=Tb: e.tensor_copy(
                rowAv[0:1, 0:Tb], pA[0:1, 0:Tb]), waits=(tokA,) + sl_rowA.write_waits())
            sl_rowA.wrote(tA)
            tB = b.emit("vector", lambda e, pB=pB, Tb=Tb: e.tensor_copy(
                rowBv[0:1, 0:Tb], pB[0:1, 0:Tb]), waits=(tokB,) + sl_rowB.write_waits())
            sl_rowB.wrote(tB)
            pool.release(iA, tA)
            pool.release(iB, tB)
            tmm, trs = ln_from_rows(Tb, rowAv, rowBv, lnm, lnr, lnt,
                                    sl_rowA, sl_rowB, sl_lnt, extra=(tA, tB))
            mb = lnm[:, 0:Tb].unsqueeze(1).broadcast_to((128, NCH, Tb))
            rb = lnr[:, 0:Tb].unsqueeze(1).broadcast_to((128, NCH, Tb))
            h2w = sl_x2.write_waits()
            t7 = b.emit("vector", lambda e, Tb=Tb, mb=mb: e.tensor_tensor(
                x2f[:, :, 0:Tb], x2f[:, :, 0:Tb], mb, ALU.subtract),
                waits=(tmm, trs) + h2w)
            b.emit("vector", lambda e, Tb=Tb, rb=rb: e.tensor_tensor(
                x2f[:, :, 0:Tb], x2f[:, :, 0:Tb], rb, ALU.mult))
            th = None
            for j in range(NCH):
                th = b.emit("vector", lambda e, j=j, Tb=Tb: e.tensor_scalar(
                    x2f[:, j, 0:Tb], x2f[:, j, 0:Tb], cons["g2"][:, j:j + 1],
                    cons["b2"][:, j:j + 1], ALU.mult, ALU.add))
            sl_x2.wrote(th)
            dw_ = sl_d2.write_waits()
            td1 = b.emit("vector", lambda e, Tb=Tb: e.tensor_tensor(
                dd3[:, :, 1:Tb], x2f[:, :, 1:Tb], x2f[:, :, 0:Tb - 1], ALU.subtract), waits=dw_)
            dcol = dd3[:, :, 0:1].rearrange("p a t -> p (a t)")
            hcol = x2f[:, :, 0:1].rearrange("p a t -> p (a t)")
            td2 = b.emit("vector", lambda e: e.tensor_tensor(dcol, hcol, c_carry2, ALU.subtract))
            sl_d2.wrote(td2)
            if bi + 1 < len(TBLK):
                b.emit("vector", lambda e, Tb=Tb, bi=bi: e.tensor_scalar(
                    c_carry2, x2f[:, :, Tb - 1:Tb].rearrange("p a t -> p (a t)"),
                    c_mask[:, bi + 1:bi + 2], None, ALU.mult))
            nseg = (Tb + 255) // 256
            for n in ("xr2", "xk2"):
                for g in range(4):
                    for sgi in range(nseg):
                        t0 = sgi * 256
                        T2 = min(256, Tb - t0)
                        half = (bi * 4 * nseg + g * nseg + sgi) % 2
                        stile = stg[n][half]
                        sslot = sl_stg[n][half]
                        wfree = sslot.write_waits()
                        tl = None
                        for jj in range(4):
                            j = g * 4 + jj
                            tl = b.emit("vector", lambda e, j=j, jj=jj, t0=t0, T2=T2,
                                        st=stile, m=mu1[n]: e.scalar_tensor_tensor(
                                            st[:, jj, 0:T2], dd3[:, j, t0:t0 + T2],
                                            cons[m][:, j:j + 1], x2f[:, j, t0:t0 + T2],
                                            ALU.mult, ALU.add),
                                        waits=((td1, td2) + wfree) if jj == 0 else ())
                        sslot.wrote(tl)
                        sl_d2.used(tl); sl_x2.used(tl)
                        tsp = b.emit("gpsimd", lambda e, off=off, t0=t0, T2=T2, st=stile,
                                     sd=dsts[n], g=g: e.dma_start(
                                         cm3(sd)[:, g * 4:(g + 1) * 4, off + t0:off + t0 + T2],
                                         st[:, :, 0:T2]), waits=(tl,))
                        sslot.used(tsp)

    def p4():
        phase_open()
        wv = {n: A3(f"p4w_{n}", BF16, NCH, 512) for n in ("cwrT", "cwkT")}
        actv = {n: [A3(f"p4a_{n}{h}", BF16, NCH, 512) for h in range(2)]
                for n in ("xr2", "xk2")}
        r2st = [A(f"p4r2{h}", BF16, 512) for h in range(2)]
        k2tmp = [A(f"p4kt{h}", F32, 512) for h in range(2)]
        k2st = [A(f"p4k2{h}", BF16, 512) for h in range(2)]
        srcs = {"xr2": s_xr2, "xk2": s_xk2}
        wname = {"xr2": "cwrT", "xk2": "cwkT"}
        sl_w = {n: [Slot(), Slot()] for n in wv}
        sl_act = {n: [Slot(), Slot()] for n in actv}
        sl_r2 = [Slot(), Slot()]
        sl_kt = [Slot(), Slot()]
        sl_k2 = [Slot(), Slot()]

        for wb in range(4):
            wtoks = {0: [], 1: []}
            for hf in range(2):
                for n in ("cwrT", "cwkT"):
                    t = b.emit("gpsimd", lambda e, n=n, wb=wb, hf=hf: e.dma_start(
                        wv[n][:, :, hf * 256:(hf + 1) * 256],
                        wT[n].rearrange("(a p) d -> p a d", p=128)[
                            :, :, wb * 512 + hf * 256:wb * 512 + (hf + 1) * 256]),
                        waits=sl_w[n][hf].write_waits())
                    sl_w[n][hf].wrote(t)
                    wtoks[hf].append(t)
            wtoks = {k: tuple(v) for k, v in wtoks.items()}
            for ti, (off, Tb) in enumerate(OBLK):
                half = ti % 2
                atoks = []
                for n in ("xr2", "xk2"):
                    sl = sl_act[n][half]
                    t = b.emit("sync", lambda e, n=n, off=off, Tb=Tb, half=half: e.dma_start(
                        actv[n][half][:, :, 0:Tb], cm3(srcs[n])[:, :, off:off + Tb]),
                        waits=sl.write_waits())
                    sl.wrote(t)
                    atoks.append(t)
                atoks = tuple(atoks)
                for dj in range(4):
                    d_glob = wb * 4 + dj
                    hf = dj // 2
                    for n in ("xr2", "xk2"):
                        ii, pp, wfree = pool.acquire()
                        tok = None
                        for j in range(NCH):
                            tok = b.emit("tensor", lambda e, n=n, j=j, dj=dj, pp=pp, half=half,
                                         Tb=Tb: e.matmul(
                                             pp[:, 0:Tb],
                                             wv[wname[n]][:, j, dj * 128:(dj + 1) * 128],
                                             actv[n][half][:, j, 0:Tb],
                                             start=(j == 0), stop=(j == NCH - 1)),
                                         waits=(wfree + atoks + wtoks[hf]) if j == 0 else (),
                                         inc=(1 if j == NCH - 1 else 0))
                        sl_w[wname[n]][hf].used(tok)
                        sl_act[n][half].used(tok)
                        sh = (ti * 4 + dj) % 2
                        if n == "xr2":
                            t1 = b.emit("scalar", lambda e, pp=pp, Tb=Tb, sh=sh: e.activation(
                                r2st[sh][:, 0:Tb], pp[:, 0:Tb], AF.Sigmoid),
                                waits=(tok,) + sl_r2[sh].write_waits())
                            sl_r2[sh].wrote(t1)
                            pool.release(ii, t1)
                            tsp = b.emit_sdma(lambda e, off=off, Tb=Tb, dg=d_glob, sh=sh:
                                              e.dma_start(cm3(s_r2)[:, dg, off:off + Tb],
                                                          r2st[sh][:, 0:Tb]), waits=(t1,))
                            sl_r2[sh].used(tsp)
                        else:
                            t1 = b.emit("scalar", lambda e, pp=pp, Tb=Tb, sh=sh: e.activation(
                                k2tmp[sh][:, 0:Tb], pp[:, 0:Tb], AF.Relu),
                                waits=(tok,) + sl_kt[sh].write_waits())
                            sl_kt[sh].wrote(t1)
                            pool.release(ii, t1)
                            t2 = b.emit("vector", lambda e, Tb=Tb, sh=sh: e.tensor_tensor(
                                k2st[sh][:, 0:Tb], k2tmp[sh][:, 0:Tb], k2tmp[sh][:, 0:Tb],
                                ALU.mult), waits=(t1,) + sl_k2[sh].write_waits())
                            sl_k2[sh].wrote(t2)
                            sl_kt[sh].used(t2)
                            tsp = b.emit_sdma(lambda e, off=off, Tb=Tb, dg=d_glob, sh=sh:
                                              e.dma_start(cm3(s_k2)[:, dg, off:off + Tb],
                                                          k2st[sh][:, 0:Tb]), waits=(t2,))
                            sl_k2[sh].used(tsp)

    def p6a():
        phase_open()
        xnres = A3("p6xn", BF16, NCH, 2048)
        g8 = [A83(f"p6g8_{h}", NCH, 512) for h in range(2)]
        u8 = [A83(f"p6u8_{h}", NCH, 512) for h in range(2)]
        gbf = [A3(f"p6g_{h}", BF16, NCH, 512) for h in range(2)]
        ubf = [A3(f"p6u_{h}", BF16, NCH, 512) for h in range(2)]
        gt = [A(f"p6gt{h}", F32, 512) for h in range(2)]
        ut = [A(f"p6ut{h}", F32, 512) for h in range(2)]
        fst = [A(f"p6fst{h}", BF16, 512) for h in range(2)]
        d8 = [A8(f"p6d8_{h}", 2048) for h in range(2)]
        dbf = [A(f"p6dbf{h}", BF16, 2048) for h in range(2)]

        sl_g8 = [Slot(), Slot()]; sl_u8 = [Slot(), Slot()]
        sl_g = [Slot(), Slot()]; sl_u = [Slot(), Slot()]
        sl_gt = [Slot(), Slot()]; sl_ut = [Slot(), Slot()]
        sl_f = [Slot(), Slot()]
        sl_d8 = [Slot(), Slot()]; sl_dbf = [Slot(), Slot()]

        txn = b.emit("sync", lambda e: e.dma_start(
            xnres, cm3(s_xnbf)[:, :, HALO:HALO + 2048]))
        gw3 = gw.rearrange("(a p) o -> p a o", p=128)
        uw3 = uw.rearrange("(a p) o -> p a o", p=128)
        dw3 = dw.rearrange("(a p) h -> p a h", p=128)

        dquant_i = [0]

        def emit_dquant():
            ich = dquant_i[0]
            if ich >= NICH:
                return
            dquant_i[0] += 1
            h = ich % 2
            t1 = b.emit("gpsimd", lambda e, ich=ich, h=h: e.dma_start(
                d8[h], dw3[:, ich, :]), waits=sl_d8[h].write_waits())
            sl_d8[h].wrote(t1)
            t2 = b.emit("gpsimd", lambda e, h=h: e.dma_start(dbf[h], d8[h]),
                        waits=(t1,) + sl_dbf[h].write_waits())
            sl_dbf[h].wrote(t2); sl_d8[h].used(t2)
            t3 = b.emit("sync", lambda e, ich=ich, h=h: e.dma_start(
                s_dwq.rearrange("a p t -> p a t")[:, ich, :], dbf[h]), waits=(t2,))
            sl_dbf[h].used(t3)

        for ib in range(12):
            h = ib % 2
            tg8 = b.emit("gpsimd", lambda e, ib=ib, h=h: e.dma_start(
                g8[h], gw3[:, :, ib * 512:(ib + 1) * 512]), waits=sl_g8[h].write_waits())
            sl_g8[h].wrote(tg8)
            tg = b.emit("gpsimd", lambda e, h=h: e.dma_start(gbf[h], g8[h]),
                        waits=(tg8,) + sl_g[h].write_waits())
            sl_g[h].wrote(tg); sl_g8[h].used(tg)
            tu8 = b.emit("gpsimd", lambda e, ib=ib, h=h: e.dma_start(
                u8[h], uw3[:, :, ib * 512:(ib + 1) * 512]), waits=sl_u8[h].write_waits())
            sl_u8[h].wrote(tu8)
            tu = b.emit("gpsimd", lambda e, h=h: e.dma_start(ubf[h], u8[h]),
                        waits=(tu8,) + sl_u[h].write_waits())
            sl_u[h].wrote(tu); sl_u8[h].used(tu)
            for _ in range(4):
                emit_dquant()
            for ti, (ot, Tb) in enumerate([(0, 512), (512, 512), (1024, 512), (1536, 512)]):
                for s in range(4):
                    q = ib * 4 + s
                    iG, pG, wG = pool.acquire()
                    iU, pU, wU = pool.acquire()
                    tokG = tokU = None
                    for j in range(NCH):
                        tokG = b.emit("tensor", lambda e, j=j, s=s, pG=pG, h=h, ot=ot, Tb=Tb:
                                      e.matmul(pG[:, 0:Tb],
                                               gbf[h][:, j, s * 128:(s + 1) * 128],
                                               xnres[:, j, ot:ot + Tb],
                                               start=(j == 0), stop=(j == NCH - 1)),
                                      waits=(wG + (tg, txn)) if j == 0 else (),
                                      inc=(1 if j == NCH - 1 else 0))
                    for j in range(NCH):
                        tokU = b.emit("tensor", lambda e, j=j, s=s, pU=pU, h=h, ot=ot, Tb=Tb:
                                      e.matmul(pU[:, 0:Tb],
                                               ubf[h][:, j, s * 128:(s + 1) * 128],
                                               xnres[:, j, ot:ot + Tb],
                                               start=(j == 0), stop=(j == NCH - 1)),
                                      waits=(wU + (tu, txn)) if j == 0 else (),
                                      inc=(1 if j == NCH - 1 else 0))
                    sl_g[h].used(tokG); sl_u[h].used(tokU)
                    sh = (ti * 4 + s) % 2
                    t1 = b.emit("scalar", lambda e, pG=pG, Tb=Tb, sh=sh, q=q: e.activation(
                        gt[sh][:, 0:Tb], pG[:, 0:Tb], AF.Silu, scale=c_gs[:, q:q + 1]),
                        waits=(tokG,) + sl_gt[sh].write_waits())
                    sl_gt[sh].wrote(t1)
                    pool.release(iG, t1)
                    t2 = b.emit("scalar", lambda e, pU=pU, Tb=Tb, sh=sh, q=q: e.mul(
                        ut[sh][:, 0:Tb], pU[:, 0:Tb], c_us[:, q:q + 1]),
                        waits=(tokU,) + sl_ut[sh].write_waits())
                    sl_ut[sh].wrote(t2)
                    pool.release(iU, t2)
                    t3 = b.emit("vector", lambda e, Tb=Tb, sh=sh: e.tensor_tensor(
                        fst[sh][:, 0:Tb], gt[sh][:, 0:Tb], ut[sh][:, 0:Tb], ALU.mult),
                        waits=(t1, t2) + sl_f[sh].write_waits())
                    sl_f[sh].wrote(t3)
                    sl_gt[sh].used(t3); sl_ut[sh].used(t3)
                    tsp = b.emit("sync", lambda e, q=q, ot=ot, Tb=Tb, sh=sh: e.dma_start(
                        cm3(s_fuse)[:, q, ot:ot + Tb], fst[sh][:, 0:Tb]), waits=(t3,))
                    sl_f[sh].used(tsp)
        while dquant_i[0] < NICH:
            emit_dquant()

    def p6b():
        phase_open()
        fu = [A3(f"p6bfu{h}", BF16, NICH, 512) for h in range(2)]
        dv = [A3(f"p6bdv{h}", BF16, NICH, 256) for h in range(2)]
        fst = [A(f"p6bfst{h}", F32, 512) for h in range(2)]
        xp = [A(f"p6bxp{h}", F32, 512) for h in range(2)]
        x2p = [A(f"p6bx2p{h}", F32, 512) for h in range(2)]
        sl_fu = [Slot(), Slot()]
        sl_dv = [Slot(), Slot()]
        sl_o = [Slot(), Slot()]
        sl_xp = [Slot(), Slot()]
        sl_x2p = [Slot(), Slot()]
        for ti, (ot, Tb) in enumerate([(0, 512), (512, 512), (1024, 512), (1536, 512)]):
            fh = ti % 2
            tf = b.emit("sync", lambda e, ot=ot, Tb=Tb, fh=fh: e.dma_start(
                fu[fh][:, :, 0:Tb], cm3(s_fuse)[:, :, ot:ot + Tb]),
                waits=sl_fu[fh].write_waits())
            sl_fu[fh].wrote(tf)
            for hb in range(8):
                dh = (ti * 8 + hb) % 2
                td = b.emit("sync", lambda e, hb=hb, dh=dh: e.dma_start(
                    dv[dh], cm3(s_dwq)[:, :, hb * 256:(hb + 1) * 256]),
                    waits=sl_dv[dh].write_waits())
                sl_dv[dh].wrote(td)
                for hc in range(2):
                    d_glob = hb * 2 + hc
                    sh = (ti * 16 + hb * 2 + hc) % 2
                    txp = b.emit("sync", lambda e, dg=d_glob, ot=ot, Tb=Tb, sh=sh: e.dma_start(
                        xp[sh][:, 0:Tb], xT3[:, dg, ot + HALO:ot + HALO + Tb]),
                        waits=sl_xp[sh].write_waits())
                    sl_xp[sh].wrote(txp)
                    tx2 = b.emit("sync", lambda e, dg=d_glob, ot=ot, Tb=Tb, sh=sh: e.dma_start(
                        x2p[sh][:, 0:Tb], cm3(s_x2)[:, dg, ot + HALO:ot + HALO + Tb]),
                        waits=sl_x2p[sh].write_waits())
                    sl_x2p[sh].wrote(tx2)
                    ii, pp, wfree = pool.acquire()
                    tok = None
                    for ich in range(NICH):
                        tok = b.emit("tensor", lambda e, ich=ich, hc=hc, pp=pp, fh=fh, dh=dh,
                                     Tb=Tb: e.matmul(
                                         pp[:, 0:Tb], dv[dh][:, ich, hc * 128:(hc + 1) * 128],
                                         fu[fh][:, ich, 0:Tb],
                                         start=(ich == 0), stop=(ich == NICH - 1)),
                                     waits=(wfree + (tf, td)) if ich == 0 else (),
                                     inc=(1 if ich == NICH - 1 else 0))
                    sl_fu[fh].used(tok); sl_dv[dh].used(tok)
                    t1 = b.emit("scalar", lambda e, pp=pp, Tb=Tb, sh=sh, dg=d_glob: e.mul(
                        fst[sh][:, 0:Tb], pp[:, 0:Tb], cons["ds"][:, dg:dg + 1]),
                        waits=(tok,) + sl_o[sh].write_waits())
                    pool.release(ii, t1)
                    t2 = b.emit("vector", lambda e, Tb=Tb, sh=sh: e.tensor_tensor(
                        fst[sh][:, 0:Tb], fst[sh][:, 0:Tb], xp[sh][:, 0:Tb], ALU.add),
                        waits=(t1, txp))
                    sl_xp[sh].used(t2)
                    t3 = b.emit("vector", lambda e, Tb=Tb, sh=sh: e.tensor_tensor(
                        fst[sh][:, 0:Tb], fst[sh][:, 0:Tb], x2p[sh][:, 0:Tb], ALU.add),
                        waits=(tx2,))
                    sl_x2p[sh].used(t3)
                    sl_o[sh].wrote(t3)
                    tsp = b.emit("gpsimd", lambda e, ot=ot, Tb=Tb, dg=d_glob, sh=sh: e.dma_start(
                        cm3(s_ffn)[:, dg, ot:ot + Tb], fst[sh][:, 0:Tb]), waits=(t3,))
                    sl_o[sh].used(tsp)

    def p5():
        phase_open()
        HT = 256
        cwv = A3("p5cwv", BF16, NCH, 2048)
        k2v = [A3("p5k2x0", BF16, NCH, HT), A3("p5k2x1", BF16, NCH, HT)]
        r2v = [A3(f"p5r2{h}", BF16, NCH, HT) for h in range(2)]
        prev = [A3(f"p5pre{h}", F32, NCH, HT) for h in range(2)]
        t1v = [A(f"p5t1{h}", F32, HT) for h in range(2)]
        ov = [A(f"p5ov{h}", F32, HT) for h in range(2)]
        sl_k2 = [Slot(), Slot()]
        sl_big = {n: [Slot(), Slot()] for n in ("r2", "pre")}
        sl_t1 = [Slot(), Slot()]
        sl_o = [Slot(), Slot()]

        twq = []
        for qq in range(4):
            t = b.emit("gpsimd", lambda e, qq=qq: e.dma_start(
                cwv[:, :, qq * 512:(qq + 1) * 512],
                wT["cwvT"].rearrange("(a p) d -> p a d", p=128)[:, :, qq * 512:(qq + 1) * 512]))
            twq.append(t)

        def sub_loads(sb):
            off = OBLK[sb // 2][0] + (sb % 2) * HT
            ot = off - HALO
            sh = sb % 2
            toks = {}
            tk = b.emit("sync", lambda e, off=off, sh=sh: e.dma_start(
                k2v[sh], cm3(s_k2)[:, :, off:off + HT]), waits=sl_k2[sh].write_waits())
            sl_k2[sh].wrote(tk)
            toks["k2"] = tk
            for n, dstl, srcap, o_ in (("r2", r2v, s_r2, off), ("pre", prev, s_ffn, ot)):
                t = b.emit("sync", lambda e, dst=dstl[sh], s2=srcap, o2=o_: e.dma_start(
                    dst, cm3(s2)[:, :, o2:o2 + HT]),
                    waits=sl_big[n][sh].write_waits())
                sl_big[n][sh].wrote(t)
                toks[n] = t
            return toks

        toks_cur = sub_loads(0)
        for sb in range(8):
            off = OBLK[sb // 2][0] + (sb % 2) * HT
            ot = off - HALO
            sh = sb % 2
            kh = sb % 2
            toks = toks_cur
            if sb + 1 < 8:
                toks_cur = sub_loads(sb + 1)

            for dj in range(NCH):
                ph = dj % 2
                ii, pp, wfree = pool.acquire()
                tok = None
                for j in range(NCH):
                    tok = b.emit("tensor", lambda e, j=j, dj=dj, pp=pp, kh=kh: e.matmul(
                        pp[:, 0:HT], cwv[:, j, dj * 128:(dj + 1) * 128],
                        k2v[kh][:, j, 0:HT], start=(j == 0), stop=(j == NCH - 1)),
                        waits=(wfree + (toks["k2"], twq[dj // 4])) if j == 0 else (),
                        inc=(1 if j == NCH - 1 else 0))
                sl_k2[kh].used(tok)
                tt1 = b.emit("vector", lambda e, pp=pp, sh=sh, ph=ph, dj=dj: e.tensor_tensor(
                    t1v[ph], r2v[sh][:, dj, 0:HT], pp[:, 0:HT], ALU.mult),
                    waits=(tok, toks["r2"]) + sl_t1[ph].write_waits())
                sl_t1[ph].wrote(tt1)
                sl_big["r2"][sh].used(tt1)
                pool.release(ii, tt1)
                if debug:
                    tdbg = b.emit("sync", lambda e, dj=dj, ot=ot, ph=ph: e.dma_start(
                        cm3(s_t1)[:, dj, ot:ot + HT], t1v[ph]), waits=(tt1,))
                    sl_t1[ph].used(tdbg)
                to3 = b.emit("vector", lambda e, sh=sh, ph=ph, dj=dj: e.tensor_tensor(
                    ov[ph], prev[sh][:, dj, 0:HT], t1v[ph], ALU.add),
                    waits=(toks["pre"],) + sl_o[ph].write_waits())
                sl_big["pre"][sh].used(to3)
                sl_o[ph].wrote(to3)
                sl_t1[ph].used(to3)
                tsp = b.emit("gpsimd", lambda e, dj=dj, ot=ot, ph=ph: e.dma_start(
                    out3[:, dj, ot:ot + HT], ov[ph]), waits=(to3,))
                sl_o[ph].used(tsp)

    phase_fns = {"p1": p1, "p2": p2, "p3": p3, "p4": p4,
                 "p6a": p6a, "p6b": p6b, "p5": p5}
    for ph in phases:
        b.phase = ph
        phase_fns[ph]()
        b.fence()
    phase_open()
    b.fence()

    sems = {sn: nc.alloc_semaphore(name="sm_" + sn) for sn in b.semnames}
    with nc.Block() as block:
        b.realize(nc, block, sems)
    return nc


def _prep_inputs(inputs):
    f32 = np.float32
    x = np.ascontiguousarray(np.asarray(inputs["L0_attn_x"], dtype=f32)).reshape(16384, D)

    def t(w):
        return np.ascontiguousarray(np.asarray(w, dtype=f32).T)

    shared = {
        "wrT": t(inputs["L0_attn_tm_Wr"]), "wkT": t(inputs["L0_attn_tm_Wk"]),
        "wvT": t(inputs["L0_attn_tm_Wv"]), "woT": t(inputs["L0_attn_tm_Wout"]),
        "cwrT": t(inputs["L0_attn_cm_Wr"]), "cwkT": t(inputs["L0_attn_cm_Wk"]),
        "cwvT": t(inputs["L0_attn_cm_Wv"]),
        "gw": np.ascontiguousarray(np.asarray(inputs["L0_ffn_gate_w"], dtype=f32)),
        "uw": np.ascontiguousarray(np.asarray(inputs["L0_ffn_up_w"], dtype=f32)),
        "dw": np.ascontiguousarray(np.asarray(inputs["L0_ffn_down_w"], dtype=f32)),
    }

    def v16(name):
        return np.ascontiguousarray(np.asarray(inputs[name], dtype=f32).reshape(NCH, 128).T)

    shared.update({
        "i_g1": v16("L0_attn_ln1_g"), "i_b1": v16("L0_attn_ln1_b"),
        "i_g2": v16("L0_attn_ln2_g"), "i_b2": v16("L0_attn_ln2_b"),
        "i_mur": v16("L0_attn_tm_mu_r"), "i_muk": v16("L0_attn_tm_mu_k"),
        "i_muv": v16("L0_attn_tm_mu_v"), "i_uu": v16("L0_attn_tm_u"),
        "i_cmur": v16("L0_attn_cm_mu_r"), "i_cmuk": v16("L0_attn_cm_mu_k"),
        "i_ds": v16("L0_ffn_down_scale"),
        "i_gs": np.ascontiguousarray(
            np.asarray(inputs["L0_ffn_gate_scale"], dtype=f32).reshape(NICH, 128).T),
        "i_us": np.ascontiguousarray(
            np.asarray(inputs["L0_ffn_up_scale"], dtype=f32).reshape(NICH, 128).T),
        "i_onescol": np.ones((128, 1), dtype=ml_dtypes.bfloat16),
        "i_onesrow": np.ones((1, 128), dtype=f32),
        "i_eps": np.full((128, 1), EPS, dtype=f32),
    })

    in_maps = []
    for c in range(8):
        start = c * NOUT
        if c % 4 == 0:
            halo = x[start:start + HALO]
        else:
            halo = x[start - HALO:start]
        xs = np.concatenate([halo, x[start:start + NOUT]], axis=0)
        m = np.ones((128, 8), dtype=f32)
        m[:, 0] = 0.0
        if c % 4 == 0:
            m[:, 1] = 0.0
        im = dict(shared)
        im["xT"] = np.ascontiguousarray(xs.T)
        im["i_mask"] = m
        in_maps.append(im)
    return in_maps


def kernel(**inputs):
    in_maps = _prep_inputs(inputs)
    nc = build_nc()
    res = run_bass_kernel_spmd(nc, in_maps, core_ids=list(range(8)))
    shards = [np.ascontiguousarray(np.asarray(r["out"], dtype=np.float32)).T
              for r in res.results]
    return np.concatenate(shards, axis=0).reshape(2, 8192, D).astype(np.float32)


# version 13
# speedup vs baseline: 1.0069x; 1.0069x over previous
import numpy as np
import ml_dtypes
import concourse.bass as bass
import concourse.mybir as mybir
from concourse.bass_utils import run_bass_kernel_spmd

F32 = mybir.dt.float32
BF16 = mybir.dt.bfloat16
FP8 = mybir.dt.float8e4
U8 = mybir.dt.uint8
AF = mybir.ActivationFunctionType
ALU = mybir.AluOpType

D = 2048
NCH = 16
HALO = 128
NTOK = 2176
NOUT = 2048
NI = 6144
NICH = 48
TBLK = [(0, 128), (128, 512), (640, 512), (1152, 512), (1664, 512)]
OBLK = [(128, 512), (640, 512), (1152, 512), (1664, 512)]
LN1E8 = 18.420680743952367
EPS = 1e-5

ENGS = ("sync", "gpsimd", "tensor", "vector", "scalar")
DMA_ENGS = ("sync", "gpsimd")


NDMASEM = 12


class B:

    def __init__(self):
        self.q = {e: [] for e in ENGS}
        self.cnt = {}
        self.owner = {}
        self.semnames = []
        for e in ENGS:
            if e in DMA_ENGS:
                for i in range(NDMASEM):
                    sn = f"{e}{i}"
                    self.cnt[sn] = 0
                    self.semnames.append(sn)
            else:
                self.cnt[e] = 0
                self.owner[e] = e
                self.semnames.append(e)
        self.rr = {e: 0 for e in DMA_ENGS}
        self.phase = "p0"

    def emit_sdma(self, fn, waits=()):
        ws = tuple(w for w in waits if w is not None)
        i = self.rr.get("sdma", 0)
        self.rr["sdma"] = (i + 1) % 4
        sn = f"sdma{i}"
        if sn not in self.cnt:
            self.cnt[sn] = 0
            self.semnames.append(sn)
        prev = self.cnt[sn]
        if prev:
            ws = ws + ((sn, prev),)
        self.cnt[sn] = prev + 16
        self.q["scalar"].append((ws, fn, (sn, 16), self.phase, True))
        return (sn, prev + 16)

    def emit(self, e, fn, waits=(), inc=None):
        ws = tuple(w for w in waits if w is not None)
        if e in DMA_ENGS:
            i = self.rr[e]
            self.rr[e] = (i + 1) % NDMASEM
            sn = f"{e}{i}"
            prev = self.cnt[sn]
            if prev:
                ws = ws + ((sn, prev),)
            self.cnt[sn] = prev + 16
            self.q[e].append((ws, fn, (sn, 16), self.phase, False))
            return (sn, prev + 16)
        if inc == 0:
            self.q[e].append((ws, fn, None, self.phase, False))
            return None
        self.cnt[e] += 1
        self.q[e].append((ws, fn, (e, 1), self.phase, False))
        return (e, self.cnt[e])

    def now(self):
        return tuple((sn, self.cnt[sn]) for sn in self.semnames if self.cnt[sn] > 0)

    def fence(self):
        toks = self.now()
        for e in ENGS:
            self.q[e].append((toks, None, None, self.phase, False))

    def realize(self, nc, block, sems):
        for e in ENGS:
            def make(ename):
                def body(eng):
                    seen = {sn: 0 for sn in self.semnames}
                    cur = None
                    scope = None
                    for ws, fn, inc, ph, force in self.q[ename]:
                        if ph != cur:
                            if scope is not None:
                                scope.__exit__(None, None, None)
                            scope = nc.named_scope(f"{ph}")
                            scope.__enter__()
                            cur = ph
                        for (we, wv) in ws:
                            if (not force and self.owner.get(we) == ename) or wv <= seen[we]:
                                continue
                            eng.wait_ge(sems[we], wv)
                            seen[we] = wv
                        if fn is None:
                            continue
                        inst = fn(eng)
                        if inc is not None:
                            inst.then_inc(sems[inc[0]], inc[1])
                    if scope is not None:
                        scope.__exit__(None, None, None)
                return body
            getattr(block, e)(make(e))


class Slot:

    def __init__(self):
        self.wtok = None
        self.users = []

    def write_waits(self):
        w = tuple(self.users) + ((self.wtok,) if self.wtok is not None else ())
        self.users = []
        return w

    def wrote(self, tok):
        if tok is not None:
            self.wtok = tok

    def rd(self):
        return self.wtok

    def used(self, tok):
        if tok is not None:
            self.users.append(tok)


class PsumPool:
    def __init__(self, tiles):
        self.tiles = tiles
        self.frees = [[] for _ in tiles]
        self.held = set()
        self.rr = 0

    def acquire(self):
        for _ in range(len(self.tiles)):
            i = self.rr
            self.rr = (self.rr + 1) % len(self.tiles)
            if i not in self.held:
                self.held.add(i)
                waits = tuple(self.frees[i])
                self.frees[i] = []
                return i, self.tiles[i], waits
        raise RuntimeError("all psum banks held")

    def release(self, i, tok):
        self.held.discard(i)
        if tok is not None:
            self.frees[i].append(tok)


def build_nc(debug=False, phases=("p1", "p2", "p3", "p4", "p6a", "p6b", "p5")):
    nc = bass.Bass("TRN2", target_bir_lowering=False, detect_race_conditions=False)

    def din(name, shape, dt=F32):
        return nc.dram_tensor(name, list(shape), dt, kind="ExternalInput").ap()

    xT = din("xT", (D, NTOK))
    wTn = ("wrT", "wkT", "wvT", "woT", "cwrT", "cwkT", "cwvT")
    wT = {n: din(n, (D, D)) for n in wTn}
    gw = din("gw", (D, NI))
    uw = din("uw", (D, NI))
    dw = din("dw", (NI, D))
    v16names = ("g1", "b1", "g2", "b2", "mur", "muk", "muv", "uu", "cmur", "cmuk", "ds")
    v16 = {n: din("i_" + n, (128, NCH)) for n in v16names}
    gs_i = din("i_gs", (128, NICH))
    us_i = din("i_us", (128, NICH))
    mask_i = din("i_mask", (128, 8))
    onescol_i = din("i_onescol", (128, 1), BF16)
    onesrow_i = din("i_onesrow", (1, 128))
    eps_i = din("i_eps", (128, 1))
    out = nc.dram_tensor("out", [D, NOUT], F32, kind="ExternalOutput").ap()

    skind = "ExternalOutput" if debug else "Internal"

    def scr(name, shape, dt):
        return nc.dram_tensor(name, list(shape), dt, kind=skind).ap()

    s_xn32 = scr("s_xn32", (NCH, 128, NTOK), F32)
    s_xnbf = scr("s_xnbf", (NCH, 128, NTOK), BF16)
    s_xr = scr("s_xr", (NCH, 128, NTOK), BF16)
    s_xk = scr("s_xk", (NCH, 128, NTOK), BF16)
    s_xv = scr("s_xv", (NCH, 128, NTOK), BF16)
    s_a = scr("s_a", (NCH, 128, NTOK), BF16)
    s_x2 = scr("s_x2", (NCH, 128, NTOK), F32)
    s_xr2 = scr("s_xr2", (NCH, 128, NTOK), BF16)
    s_xk2 = scr("s_xk2", (NCH, 128, NTOK), BF16)
    s_r2 = scr("s_r2", (NCH, 128, NTOK), BF16)
    s_k2 = scr("s_k2", (NCH, 128, NTOK), BF16)
    s_fuse = scr("s_fuse", (NICH, 128, NOUT), BF16)
    s_dwq = scr("s_dwq", (NICH, 128, D), BF16)
    s_ffn = scr("s_ffn", (NCH, 128, NOUT), F32)
    s_t1 = scr("s_t1", (NCH, 128, NOUT), F32) if debug else None

    def cm3(ap):
        return ap.rearrange("a p t -> p a t")

    xT3 = xT.rearrange("(a p) t -> p a t", p=128)
    out3 = out.rearrange("(a p) t -> p a t", p=128)

    import contextlib as _ctx
    _ph_stack = [None]

    def phase_open():
        if _ph_stack[0] is not None:
            _ph_stack[0].close()
        _ph_stack[0] = _ctx.ExitStack()

    def A(name, dt, cols, parts=128):
        return _ph_stack[0].enter_context(
            nc.sbuf_tensor(name, [parts, cols], dt)).ap()

    def A3(name, dt, nch, cols):
        return A(name, dt, nch * cols).rearrange("p (a t) -> p a t", a=nch)

    def A8(name, cols):
        u = _ph_stack[0].enter_context(nc.sbuf_tensor(name, [128, cols], U8)).ap()
        return u.bitcast(FP8)

    def A83(name, nch, cols):
        return A8(name, nch * cols).rearrange("p (a t) -> p a t", a=nch)

    cons = {}
    for n in v16names + ("murm1", "mukm1", "muvm1", "cmurm1", "cmukm1", "uprime"):
        cons[n] = nc.alloc_sbuf_tensor("c_" + n, [128, NCH], F32).ap()
    c_gs = nc.alloc_sbuf_tensor("c_gs", [128, NICH], F32).ap()
    c_us = nc.alloc_sbuf_tensor("c_us", [128, NICH], F32).ap()
    c_mask = nc.alloc_sbuf_tensor("c_mask", [128, 8], F32).ap()
    c_ones_bf = nc.alloc_sbuf_tensor("c_ones_bf", [128, 1], BF16).ap()
    c_onesrow = nc.alloc_sbuf_tensor("c_onesrow", [1, 128], F32).ap()
    c_eps = nc.alloc_sbuf_tensor("c_eps", [128, 1], F32).ap()
    c_carry = nc.alloc_sbuf_tensor("c_carry", [128, NCH], F32).ap()
    c_carry2 = nc.alloc_sbuf_tensor("c_carry2", [128, NCH], F32).ap()
    c_tcol = nc.alloc_sbuf_tensor("c_tcol", [128, NCH], F32).ap()

    EPS_AP = c_eps[:, 0:1]
    psum_t = [nc.alloc_psum_tensor(f"psb{i}", [128, 512], F32).ap() for i in range(8)]

    b = B()
    pool = PsumPool(psum_t)

    ld = []
    for n in v16names:
        ld.append(b.emit("sync", lambda e, d=cons[n], s=v16[n]: e.dma_start(d, s)))
    for d_, s_ in ((c_gs, gs_i), (c_us, us_i), (c_mask, mask_i), (c_ones_bf, onescol_i),
                   (c_onesrow, onesrow_i), (c_eps, eps_i)):
        ld.append(b.emit("sync", lambda e, d=d_, s=s_: e.dma_start(d, s)))
    allld = tuple(ld)
    for srcn, dstn in (("mur", "murm1"), ("muk", "mukm1"), ("muv", "muvm1"),
                       ("cmur", "cmurm1"), ("cmuk", "cmukm1")):
        b.emit("vector", lambda e, s=cons[srcn], d=cons[dstn]:
               e.tensor_scalar(d, s, -1.0, None, ALU.add), waits=allld)
    b.emit("vector", lambda e: e.tensor_scalar(cons["uprime"], cons["uu"], LN1E8, None, ALU.add),
           waits=allld)
    b.emit("vector", lambda e: e.memset(c_carry, 0.0))
    b.emit("vector", lambda e: e.memset(c_carry2, 0.0))
    b.emit("vector", lambda e: e.memset(c_tcol, 0.0))
    b.fence()

    def ln_from_rows(Tb, rowA, rowB, lnm, lnr, lnt, slA, slB, sl_lnt, extra=()):
        iS, pS, wS = pool.acquire()
        iQ, pQ, wQ = pool.acquire()
        tS = b.emit("tensor", lambda e, p=pS: e.matmul(p[:, 0:Tb], c_onesrow, rowA[0:1, 0:Tb],
                                                       start=True, stop=True),
                    waits=wS + extra)
        slA.used(tS)
        tQ = b.emit("tensor", lambda e, p=pQ: e.matmul(p[:, 0:Tb], c_onesrow, rowB[0:1, 0:Tb],
                                                       start=True, stop=True),
                    waits=wQ + extra)
        slB.used(tQ)
        tm = b.emit("vector", lambda e, p=pS: e.tensor_scalar(lnm[:, 0:Tb], p[:, 0:Tb],
                                                              1.0 / D, None, ALU.mult),
                    waits=(tS,))
        tq = b.emit("vector", lambda e, p=pQ: e.tensor_scalar(lnt[:, 0:Tb], p[:, 0:Tb],
                                                              1.0 / D, None, ALU.mult),
                    waits=(tQ,) + sl_lnt.write_waits())
        pool.release(iS, tm)
        pool.release(iQ, tq)
        b.emit("vector", lambda e: e.tensor_tensor(lnr[:, 0:Tb], lnm[:, 0:Tb], lnm[:, 0:Tb],
                                                   ALU.mult))
        t2 = b.emit("vector", lambda e: e.tensor_tensor(lnt[:, 0:Tb], lnt[:, 0:Tb], lnr[:, 0:Tb],
                                                        ALU.subtract))
        t3 = b.emit("scalar", lambda e: e.activation(lnt[:, 0:Tb], lnt[:, 0:Tb], AF.Sqrt,
                                                     bias=EPS_AP, scale=1.0), waits=(t2,))
        sl_lnt.wrote(t3)
        t4 = b.emit("vector", lambda e: e.reciprocal(lnr[:, 0:Tb], lnt[:, 0:Tb]), waits=(t3,))
        sl_lnt.used(t4)
        return tm, t4

    def colsum_pair(Tb, vbf, vsq, rdy, rowA, rowB, slA, slB):
        toks = []
        for vv, row, sl in ((vbf, rowA, slA), (vsq, rowB, slB)):
            ii, pp, wf = pool.acquire()
            tok = None
            for j in range(NCH):
                tok = b.emit("tensor", lambda e, p=pp, v=vv, j=j, Tb=Tb: e.matmul(
                    p[0:1, 0:Tb], c_ones_bf, v[:, j, 0:Tb],
                    start=(j == 0), stop=(j == NCH - 1)),
                    waits=(wf + rdy) if j == 0 else (), inc=(1 if j == NCH - 1 else 0))
            tc = b.emit("vector", lambda e, p=pp, r=row, Tb=Tb: e.tensor_copy(
                r[0:1, 0:Tb], p[0:1, 0:Tb]), waits=(tok,) + sl.write_waits())
            sl.wrote(tc)
            pool.release(ii, tc)
            toks.append(tc)
        return tuple(toks)

    def p1():
        phase_open()
        x3 = A3("p1x", F32, NCH, 512)
        xn3 = A3("p1xn", F32, NCH, 512)
        dd3 = A3("p1d", F32, NCH, 512)
        h3 = A3("p1h", F32, NCH, 512)
        xbf3 = A3("p1xbf", BF16, NCH, 512)
        xsq3 = A3("p1xsq", BF16, NCH, 512)
        stg = {n: [A3(f"p1stg_{n}{h}", BF16, 4, 512) for h in range(2)]
               for n in ("xr", "xk", "xv")}
        lnm = A("p1lnm", F32, 512)
        lnr = A("p1lnr", F32, 512)
        lnt = A("p1lnt", F32, 512)
        rowAv = A("p1_rowA", F32, 512, parts=1)
        rowBv = A("p1_rowB", F32, 512, parts=1)
        sl_rowA = Slot(); sl_rowB = Slot(); sl_lnt = Slot()

        sl_x = Slot(); sl_xn = Slot(); sl_hp = Slot(); sl_h = Slot()
        sl_xbf = Slot(); sl_xsq = Slot()
        sl_stg = {n: [Slot(), Slot()] for n in stg}
        dsts = {"xr": s_xr, "xk": s_xk, "xv": s_xv}
        mu1 = {"xr": "murm1", "xk": "mukm1", "xv": "muvm1"}

        for bi, (off, Tb) in enumerate(TBLK):
            txl = b.emit("sync", lambda e, off=off, Tb=Tb: e.dma_start(
                x3[:, :, 0:Tb], xT3[:, :, off:off + Tb]), waits=sl_x.write_waits())
            sl_x.wrote(txl)
            txb = b.emit("scalar", lambda e, Tb=Tb: e.copy(xbf3[:, :, 0:Tb], x3[:, :, 0:Tb]),
                         waits=(txl,) + sl_xbf.write_waits())
            sl_xbf.wrote(txb); sl_x.used(txb)
            txq = b.emit("scalar", lambda e, Tb=Tb: e.activation(xsq3[:, :, 0:Tb], x3[:, :, 0:Tb],
                                                                 AF.Square),
                         waits=(txl,) + sl_xsq.write_waits())
            sl_xsq.wrote(txq); sl_x.used(txq)
            tA, tB = colsum_pair(Tb, xbf3, xsq3, (txb, txq), rowAv, rowBv, sl_rowA, sl_rowB)
            sl_xbf.used(tA); sl_xsq.used(tB)
            tmm, trs = ln_from_rows(Tb, rowAv, rowBv, lnm, lnr, lnt,
                                    sl_rowA, sl_rowB, sl_lnt, extra=(tA, tB))
            mb = lnm[:, 0:Tb].unsqueeze(1).broadcast_to((128, NCH, Tb))
            rb = lnr[:, 0:Tb].unsqueeze(1).broadcast_to((128, NCH, Tb))
            t5 = b.emit("vector", lambda e, Tb=Tb, mb=mb: e.tensor_tensor(
                xn3[:, :, 0:Tb], x3[:, :, 0:Tb], mb, ALU.subtract),
                waits=(tmm, trs, txl) + sl_xn.write_waits())
            sl_x.used(t5)
            t6 = b.emit("vector", lambda e, Tb=Tb, rb=rb: e.tensor_tensor(
                xn3[:, :, 0:Tb], xn3[:, :, 0:Tb], rb, ALU.mult))
            sl_xn.wrote(t6)
            sp1 = b.emit("gpsimd", lambda e, off=off, Tb=Tb: e.dma_start(
                cm3(s_xn32)[:, :, off:off + Tb], xn3[:, :, 0:Tb]), waits=(t6,))
            sl_xn.used(sp1)
            txnb = b.emit("scalar", lambda e, Tb=Tb: e.copy(xbf3[:, :, 0:Tb], xn3[:, :, 0:Tb]),
                          waits=(t6,) + sl_xbf.write_waits())
            sl_xbf.wrote(txnb); sl_xn.used(txnb)
            sp2 = b.emit("gpsimd", lambda e, off=off, Tb=Tb: e.dma_start(
                cm3(s_xnbf)[:, :, off:off + Tb], xbf3[:, :, 0:Tb]), waits=(txnb,))
            sl_xbf.used(sp2)
            txq2 = b.emit("scalar", lambda e, Tb=Tb: e.activation(
                xsq3[:, :, 0:Tb], xn3[:, :, 0:Tb], AF.Square), waits=(t6,) + sl_xsq.write_waits())
            sl_xsq.wrote(txq2); sl_xn.used(txq2)
            tA2, tB2 = colsum_pair(Tb, xbf3, xsq3, (txnb, txq2), rowAv, rowBv, sl_rowA, sl_rowB)
            sl_xbf.used(tA2); sl_xsq.used(tB2)
            tmm2, trs2 = ln_from_rows(Tb, rowAv, rowBv, lnm, lnr, lnt,
                                      sl_rowA, sl_rowB, sl_lnt, extra=(tA2, tB2))
            mb2 = lnm[:, 0:Tb].unsqueeze(1).broadcast_to((128, NCH, Tb))
            rb2 = lnr[:, 0:Tb].unsqueeze(1).broadcast_to((128, NCH, Tb))
            hw = sl_h.write_waits()
            t7 = b.emit("vector", lambda e, Tb=Tb, mb2=mb2: e.tensor_tensor(
                h3[:, :, 0:Tb], xn3[:, :, 0:Tb], mb2, ALU.subtract),
                waits=(tmm2, trs2) + hw)
            sl_xn.used(t7)
            b.emit("vector", lambda e, Tb=Tb, rb2=rb2: e.tensor_tensor(
                h3[:, :, 0:Tb], h3[:, :, 0:Tb], rb2, ALU.mult))
            th = None
            for j in range(NCH):
                th = b.emit("vector", lambda e, j=j, Tb=Tb: e.tensor_scalar(
                    h3[:, j, 0:Tb], h3[:, j, 0:Tb], cons["g1"][:, j:j + 1],
                    cons["b1"][:, j:j + 1], ALU.mult, ALU.add))
            sl_h.wrote(th)
            dw_ = sl_hp.write_waits()
            td1 = b.emit("vector", lambda e, Tb=Tb: e.tensor_tensor(
                dd3[:, :, 1:Tb], h3[:, :, 1:Tb], h3[:, :, 0:Tb - 1], ALU.subtract), waits=dw_)
            dcol = dd3[:, :, 0:1].rearrange("p a t -> p (a t)")
            hcol = h3[:, :, 0:1].rearrange("p a t -> p (a t)")
            td2 = b.emit("vector", lambda e: e.tensor_tensor(dcol, hcol, c_tcol, ALU.subtract))
            sl_hp.wrote(td2)
            if bi + 1 < len(TBLK):
                b.emit("vector", lambda e, Tb=Tb, bi=bi: e.tensor_scalar(
                    c_tcol, h3[:, :, Tb - 1:Tb].rearrange("p a t -> p (a t)"),
                    c_mask[:, bi + 1:bi + 2], None, ALU.mult))
            for n in ("xr", "xk", "xv"):
                for g in range(4):
                    half = (bi * 4 + g) % 2
                    stile = stg[n][half]
                    sslot = sl_stg[n][half]
                    wfree = sslot.write_waits()
                    tl = None
                    for jj in range(4):
                        j = g * 4 + jj
                        tl = b.emit("vector", lambda e, j=j, jj=jj, Tb=Tb, st=stile, m=mu1[n]:
                                    e.scalar_tensor_tensor(
                                        st[:, jj, 0:Tb], dd3[:, j, 0:Tb],
                                        cons[m][:, j:j + 1], h3[:, j, 0:Tb],
                                        ALU.mult, ALU.add),
                                    waits=((td1, td2) + wfree) if jj == 0 else ())
                    sslot.wrote(tl)
                    sl_hp.used(tl); sl_h.used(tl)
                    tsp = b.emit("gpsimd", lambda e, off=off, Tb=Tb, st=stile, sd=dsts[n], g=g:
                                 e.dma_start(cm3(sd)[:, g * 4:(g + 1) * 4, off:off + Tb],
                                             st[:, :, 0:Tb]), waits=(tl,))
                    sslot.used(tsp)

    def p2():
        phase_open()
        wv = {n: A3(f"p2w_{n}", BF16, NCH, 512)
              for n in ("wrT", "wkT", "wvT")}
        actv = {n: [A3(f"p2a_{n}{h}", BF16, NCH, 512) for h in range(2)]
                for n in ("xr", "xk", "xv")}
        astg = [A3(f"p2astg{h}", BF16, 4, 512) for h in range(2)]
        rt = [A(f"p2rt{h}", F32, 512) for h in range(2)]
        ft = [A(f"p2ft{h}", F32, 512) for h in range(2)]
        srcs = {"xr": s_xr, "xk": s_xk, "xv": s_xv}
        wname = {"xr": "wrT", "xk": "wkT", "xv": "wvT"}

        sl_w = {n: [Slot(), Slot()] for n in wv}
        sl_act = {n: [Slot(), Slot()] for n in actv}
        sl_ast = [Slot(), Slot()]
        sl_r = [Slot(), Slot()]
        sl_f = [Slot(), Slot()]

        for wb in range(4):
            wtoks = {0: [], 1: []}
            for hf in range(2):
                for n in ("xr", "xk", "xv"):
                    wn = wname[n]
                    t = b.emit("gpsimd", lambda e, wn=wn, wb=wb, hf=hf: e.dma_start(
                        wv[wn][:, :, hf * 256:(hf + 1) * 256],
                        wT[wn].rearrange("(a p) d -> p a d", p=128)[
                            :, :, wb * 512 + hf * 256:wb * 512 + (hf + 1) * 256]),
                        waits=sl_w[wn][hf].write_waits())
                    sl_w[wn][hf].wrote(t)
                    wtoks[hf].append(t)
            wtoks = {k: tuple(v) for k, v in wtoks.items()}
            for ti, (off, Tb) in enumerate(TBLK):
                half = ti % 2
                atoks = []
                for n in ("xr", "xk", "xv"):
                    sl = sl_act[n][half]
                    t = b.emit("sync", lambda e, n=n, off=off, Tb=Tb, half=half: e.dma_start(
                        actv[n][half][:, :, 0:Tb], cm3(srcs[n])[:, :, off:off + Tb]),
                        waits=sl.write_waits())
                    sl.wrote(t)
                    atoks.append(t)
                atoks = tuple(atoks)
                for dj in range(4):
                    d_glob = wb * 4 + dj
                    hf = dj // 2
                    ptoks = {}
                    banks = {}
                    for n in ("xr", "xk", "xv"):
                        ii, pp, wfree = pool.acquire()
                        banks[n] = (ii, pp)
                        tok = None
                        for j in range(NCH):
                            tok = b.emit("tensor", lambda e, n=n, j=j, dj=dj, pp=pp, half=half,
                                         Tb=Tb: e.matmul(
                                             pp[:, 0:Tb],
                                             wv[wname[n]][:, j, dj * 128:(dj + 1) * 128],
                                             actv[n][half][:, j, 0:Tb],
                                             start=(j == 0), stop=(j == NCH - 1)),
                                         waits=(wfree + atoks + wtoks[hf]) if j == 0 else (),
                                         inc=(1 if j == NCH - 1 else 0))
                        ptoks[n] = tok
                        sl_w[wname[n]][hf].used(tok)
                        sl_act[n][half].used(tok)
                    ah = (ti * 4 + dj) % 2
                    tr = b.emit("scalar", lambda e, pp=banks["xr"][1], Tb=Tb, ah=ah:
                                e.activation(rt[ah][:, 0:Tb], pp[:, 0:Tb], AF.Sigmoid),
                                waits=(ptoks["xr"],) + sl_r[ah].write_waits())
                    sl_r[ah].wrote(tr)
                    tf = b.emit("scalar", lambda e, pp=banks["xk"][1], Tb=Tb, ah=ah, dg=d_glob:
                                e.activation(ft[ah][:, 0:Tb], pp[:, 0:Tb], AF.Sigmoid,
                                             bias=cons["uprime"][:, dg:dg + 1]),
                                waits=(ptoks["xk"],) + sl_f[ah].write_waits())
                    sl_f[ah].wrote(tf)
                    pool.release(banks["xr"][0], tr)
                    pool.release(banks["xk"][0], tf)
                    tv1 = b.emit("vector", lambda e, pp=banks["xv"][1], Tb=Tb, ah=ah:
                                 e.tensor_tensor(ft[ah][:, 0:Tb], ft[ah][:, 0:Tb], pp[:, 0:Tb],
                                                 ALU.mult), waits=(ptoks["xv"], tf))
                    sl_f[ah].wrote(tv1)
                    pool.release(banks["xv"][0], tv1)
                    ast = sl_ast[half]
                    wfree = ast.write_waits() if dj == 0 else ()
                    ta = b.emit("vector", lambda e, Tb=Tb, ah=ah, half=half, dj=dj:
                                e.tensor_tensor(astg[half][:, dj, 0:Tb], rt[ah][:, 0:Tb],
                                                ft[ah][:, 0:Tb], ALU.mult),
                                waits=(tv1, tr) + wfree)
                    ast.wrote(ta)
                    sl_r[ah].used(ta)
                    sl_f[ah].used(ta)
                    if dj == 3:
                        tsp = b.emit_sdma(lambda e, off=off, Tb=Tb, wb=wb, half=half:
                                          e.dma_start(cm3(s_a)[:, wb * 4:(wb + 1) * 4, off:off + Tb],
                                                      astg[half][:, :, 0:Tb]), waits=(ta,))
                        ast.used(tsp)

    def p3():
        phase_open()
        wvt = A3("p3wvt", BF16, NCH, 2048)
        av = A3("p3av", BF16, NCH, 512)
        _xn = A3("p3xn", F32, 8, 512)
        xnv = [_xn, _xn]
        x2f = A3("p3x2", F32, NCH, 512)
        dd3 = A3("p3d", BF16, NCH, 512)
        stg = {n: [A3(f"p3stg_{n}{h}", BF16, 4, 256) for h in range(2)]
               for n in ("xr2", "xk2")}
        lnm = A("p3lnm", F32, 512)
        lnr = A("p3lnr", F32, 512)
        lnt = A("p3lnt", F32, 512)
        x2bf = A3("p3x2bf", BF16, NCH, 512)
        x2sq = A3("p3x2sq", BF16, NCH, 512)
        rowAv = A("p3_rowA", F32, 512, parts=1)
        rowBv = A("p3_rowB", F32, 512, parts=1)
        sl_rowA = Slot(); sl_rowB = Slot(); sl_lnt = Slot()
        sl_w = [Slot(), Slot()]
        sl_a = Slot()
        _slxn = Slot()
        sl_xn = [_slxn, _slxn]
        sl_x2 = Slot()
        sl_d2 = Slot()
        sl_bf = Slot()
        sl_sq = Slot()
        sl_stg = {n: [Slot(), Slot()] for n in stg}
        dsts = {"xr2": s_xr2, "xk2": s_xk2}
        mu1 = {"xr2": "cmurm1", "xk2": "cmukm1"}

        twh = []
        for hf in range(2):
            t = b.emit("gpsimd", lambda e, hf=hf: e.dma_start(
                wvt[:, :, hf * 1024:(hf + 1) * 1024],
                wT["woT"].rearrange("(a p) d -> p a d", p=128)[
                    :, :, hf * 1024:(hf + 1) * 1024]),
                waits=sl_w[hf].write_waits())
            sl_w[hf].wrote(t)
            twh.append(t)

        for bi, (off, Tb) in enumerate(TBLK):
            ta = b.emit("sync", lambda e, off=off, Tb=Tb: e.dma_start(
                av[:, :, 0:Tb], cm3(s_a)[:, :, off:off + Tb]), waits=sl_a.write_waits())
            sl_a.wrote(ta)
            x2w = sl_x2.write_waits()
            casts = []
            for wh in range(2):
                xh = (bi * 2 + wh) % 2
                tx = b.emit("sync", lambda e, off=off, Tb=Tb, xh=xh, wh=wh: e.dma_start(
                    xnv[xh][:, :, 0:Tb], cm3(s_xn32)[:, wh * 8:(wh + 1) * 8, off:off + Tb]),
                    waits=sl_xn[xh].write_waits())
                sl_xn[xh].wrote(tx)
                for dj in range(8):
                    d_glob = wh * 8 + dj
                    hf = d_glob // 8
                    ii, pp, wfree = pool.acquire()
                    tok = None
                    for j in range(NCH):
                        tok = b.emit("tensor", lambda e, j=j, dg=d_glob, pp=pp, Tb=Tb:
                                     e.matmul(pp[:, 0:Tb], wvt[:, j, dg * 128:(dg + 1) * 128],
                                              av[:, j, 0:Tb],
                                              start=(j == 0), stop=(j == NCH - 1),
                                              skip_group_check=True),
                                     waits=(wfree + (ta, tx, twh[hf])) if j == 0 else (),
                                     inc=(1 if j == NCH - 1 else 0))
                    sl_w[hf].used(tok); sl_a.used(tok)
                    t1 = b.emit("vector", lambda e, pp=pp, Tb=Tb, xh=xh, dj=dj, dg=d_glob:
                                e.tensor_tensor(x2f[:, dg, 0:Tb], pp[:, 0:Tb],
                                                xnv[xh][:, dj, 0:Tb], ALU.add),
                                waits=(tok,) + (x2w if d_glob == 0 else ()))
                    sl_x2.wrote(t1)
                    sl_xn[xh].used(t1)
                    pool.release(ii, t1)
                    tsp = b.emit("gpsimd", lambda e, off=off, Tb=Tb, dg=d_glob: e.dma_start(
                        cm3(s_x2)[:, dg, off:off + Tb], x2f[:, dg, 0:Tb]), waits=(t1,))
                    sl_x2.used(tsp)
                t2 = b.emit("scalar", lambda e, Tb=Tb, wh=wh: e.copy(
                    x2bf[:, wh * 8:(wh + 1) * 8, 0:Tb], x2f[:, wh * 8:(wh + 1) * 8, 0:Tb]),
                    waits=(t1,) + (sl_bf.write_waits() if wh == 0 else ()))
                sl_bf.wrote(t2); sl_x2.used(t2)
                t3 = b.emit("scalar", lambda e, Tb=Tb, wh=wh: e.activation(
                    x2sq[:, wh * 8:(wh + 1) * 8, 0:Tb], x2f[:, wh * 8:(wh + 1) * 8, 0:Tb],
                    AF.Square), waits=(sl_sq.write_waits() if wh == 0 else ()))
                sl_sq.wrote(t3); sl_x2.used(t3)
                casts.append((t2, t3))
            iA, pA, wfA = pool.acquire()
            iB, pB, wfB = pool.acquire()
            tokA = tokB = None
            cA = (casts[0][0], casts[1][0])
            cB = (casts[0][1], casts[1][1])
            for dg in range(NCH):
                tokA = b.emit("tensor", lambda e, pA=pA, Tb=Tb, dg=dg: e.matmul(
                    pA[0:1, 0:Tb], c_ones_bf, x2bf[:, dg, 0:Tb],
                    start=(dg == 0), stop=(dg == 15)),
                    waits=(wfA + cA) if dg == 0 else (), inc=(1 if dg == 15 else 0))
            for dg in range(NCH):
                tokB = b.emit("tensor", lambda e, pB=pB, Tb=Tb, dg=dg: e.matmul(
                    pB[0:1, 0:Tb], c_ones_bf, x2sq[:, dg, 0:Tb],
                    start=(dg == 0), stop=(dg == 15)),
                    waits=(wfB + cB) if dg == 0 else (), inc=(1 if dg == 15 else 0))
            sl_bf.used(tokA)
            sl_sq.used(tokB)
            tA = b.emit("vector", lambda e, pA=pA, Tb=Tb: e.tensor_copy(
                rowAv[0:1, 0:Tb], pA[0:1, 0:Tb]), waits=(tokA,) + sl_rowA.write_waits())
            sl_rowA.wrote(tA)
            tB = b.emit("vector", lambda e, pB=pB, Tb=Tb: e.tensor_copy(
                rowBv[0:1, 0:Tb], pB[0:1, 0:Tb]), waits=(tokB,) + sl_rowB.write_waits())
            sl_rowB.wrote(tB)
            pool.release(iA, tA)
            pool.release(iB, tB)
            tmm, trs = ln_from_rows(Tb, rowAv, rowBv, lnm, lnr, lnt,
                                    sl_rowA, sl_rowB, sl_lnt, extra=(tA, tB))
            mb = lnm[:, 0:Tb].unsqueeze(1).broadcast_to((128, NCH, Tb))
            rb = lnr[:, 0:Tb].unsqueeze(1).broadcast_to((128, NCH, Tb))
            h2w = sl_x2.write_waits()
            t7 = b.emit("vector", lambda e, Tb=Tb, mb=mb: e.tensor_tensor(
                x2f[:, :, 0:Tb], x2f[:, :, 0:Tb], mb, ALU.subtract),
                waits=(tmm, trs) + h2w)
            b.emit("vector", lambda e, Tb=Tb, rb=rb: e.tensor_tensor(
                x2f[:, :, 0:Tb], x2f[:, :, 0:Tb], rb, ALU.mult))
            th = None
            for j in range(NCH):
                th = b.emit("vector", lambda e, j=j, Tb=Tb: e.tensor_scalar(
                    x2f[:, j, 0:Tb], x2f[:, j, 0:Tb], cons["g2"][:, j:j + 1],
                    cons["b2"][:, j:j + 1], ALU.mult, ALU.add))
            sl_x2.wrote(th)
            dw_ = sl_d2.write_waits()
            td1 = b.emit("vector", lambda e, Tb=Tb: e.tensor_tensor(
                dd3[:, :, 1:Tb], x2f[:, :, 1:Tb], x2f[:, :, 0:Tb - 1], ALU.subtract), waits=dw_)
            dcol = dd3[:, :, 0:1].rearrange("p a t -> p (a t)")
            hcol = x2f[:, :, 0:1].rearrange("p a t -> p (a t)")
            td2 = b.emit("vector", lambda e: e.tensor_tensor(dcol, hcol, c_carry2, ALU.subtract))
            sl_d2.wrote(td2)
            if bi + 1 < len(TBLK):
                b.emit("vector", lambda e, Tb=Tb, bi=bi: e.tensor_scalar(
                    c_carry2, x2f[:, :, Tb - 1:Tb].rearrange("p a t -> p (a t)"),
                    c_mask[:, bi + 1:bi + 2], None, ALU.mult))
            nseg = (Tb + 255) // 256
            for n in ("xr2", "xk2"):
                for g in range(4):
                    for sgi in range(nseg):
                        t0 = sgi * 256
                        T2 = min(256, Tb - t0)
                        half = (bi * 4 * nseg + g * nseg + sgi) % 2
                        stile = stg[n][half]
                        sslot = sl_stg[n][half]
                        wfree = sslot.write_waits()
                        tl = None
                        for jj in range(4):
                            j = g * 4 + jj
                            tl = b.emit("vector", lambda e, j=j, jj=jj, t0=t0, T2=T2,
                                        st=stile, m=mu1[n]: e.scalar_tensor_tensor(
                                            st[:, jj, 0:T2], dd3[:, j, t0:t0 + T2],
                                            cons[m][:, j:j + 1], x2f[:, j, t0:t0 + T2],
                                            ALU.mult, ALU.add),
                                        waits=((td1, td2) + wfree) if jj == 0 else ())
                        sslot.wrote(tl)
                        sl_d2.used(tl); sl_x2.used(tl)
                        tsp = b.emit("gpsimd", lambda e, off=off, t0=t0, T2=T2, st=stile,
                                     sd=dsts[n], g=g: e.dma_start(
                                         cm3(sd)[:, g * 4:(g + 1) * 4, off + t0:off + t0 + T2],
                                         st[:, :, 0:T2]), waits=(tl,))
                        sslot.used(tsp)

    def p4():
        phase_open()
        wv = {n: A3(f"p4w_{n}", BF16, NCH, 512) for n in ("cwrT", "cwkT")}
        actv = {n: [A3(f"p4a_{n}{h}", BF16, NCH, 512) for h in range(2)]
                for n in ("xr2", "xk2")}
        r2st = [A(f"p4r2{h}", BF16, 512) for h in range(2)]
        k2tmp = [A(f"p4kt{h}", F32, 512) for h in range(2)]
        k2st = [A(f"p4k2{h}", BF16, 512) for h in range(2)]
        srcs = {"xr2": s_xr2, "xk2": s_xk2}
        wname = {"xr2": "cwrT", "xk2": "cwkT"}
        sl_w = {n: [Slot(), Slot()] for n in wv}
        sl_act = {n: [Slot(), Slot()] for n in actv}
        sl_r2 = [Slot(), Slot()]
        sl_kt = [Slot(), Slot()]
        sl_k2 = [Slot(), Slot()]

        for wb in range(4):
            wtoks = {0: [], 1: []}
            for hf in range(2):
                for n in ("cwrT", "cwkT"):
                    t = b.emit("gpsimd", lambda e, n=n, wb=wb, hf=hf: e.dma_start(
                        wv[n][:, :, hf * 256:(hf + 1) * 256],
                        wT[n].rearrange("(a p) d -> p a d", p=128)[
                            :, :, wb * 512 + hf * 256:wb * 512 + (hf + 1) * 256]),
                        waits=sl_w[n][hf].write_waits())
                    sl_w[n][hf].wrote(t)
                    wtoks[hf].append(t)
            wtoks = {k: tuple(v) for k, v in wtoks.items()}
            for ti, (off, Tb) in enumerate(OBLK):
                half = ti % 2
                atoks = []
                for n in ("xr2", "xk2"):
                    sl = sl_act[n][half]
                    t = b.emit("sync", lambda e, n=n, off=off, Tb=Tb, half=half: e.dma_start(
                        actv[n][half][:, :, 0:Tb], cm3(srcs[n])[:, :, off:off + Tb]),
                        waits=sl.write_waits())
                    sl.wrote(t)
                    atoks.append(t)
                atoks = tuple(atoks)
                for dj in range(4):
                    d_glob = wb * 4 + dj
                    hf = dj // 2
                    for n in ("xr2", "xk2"):
                        ii, pp, wfree = pool.acquire()
                        tok = None
                        for j in range(NCH):
                            tok = b.emit("tensor", lambda e, n=n, j=j, dj=dj, pp=pp, half=half,
                                         Tb=Tb: e.matmul(
                                             pp[:, 0:Tb],
                                             wv[wname[n]][:, j, dj * 128:(dj + 1) * 128],
                                             actv[n][half][:, j, 0:Tb],
                                             start=(j == 0), stop=(j == NCH - 1)),
                                         waits=(wfree + atoks + wtoks[hf]) if j == 0 else (),
                                         inc=(1 if j == NCH - 1 else 0))
                        sl_w[wname[n]][hf].used(tok)
                        sl_act[n][half].used(tok)
                        sh = (ti * 4 + dj) % 2
                        if n == "xr2":
                            t1 = b.emit("scalar", lambda e, pp=pp, Tb=Tb, sh=sh: e.activation(
                                r2st[sh][:, 0:Tb], pp[:, 0:Tb], AF.Sigmoid),
                                waits=(tok,) + sl_r2[sh].write_waits())
                            sl_r2[sh].wrote(t1)
                            pool.release(ii, t1)
                            tsp = b.emit_sdma(lambda e, off=off, Tb=Tb, dg=d_glob, sh=sh:
                                              e.dma_start(cm3(s_r2)[:, dg, off:off + Tb],
                                                          r2st[sh][:, 0:Tb]), waits=(t1,))
                            sl_r2[sh].used(tsp)
                        else:
                            t1 = b.emit("scalar", lambda e, pp=pp, Tb=Tb, sh=sh: e.activation(
                                k2tmp[sh][:, 0:Tb], pp[:, 0:Tb], AF.Relu),
                                waits=(tok,) + sl_kt[sh].write_waits())
                            sl_kt[sh].wrote(t1)
                            pool.release(ii, t1)
                            t2 = b.emit("vector", lambda e, Tb=Tb, sh=sh: e.tensor_tensor(
                                k2st[sh][:, 0:Tb], k2tmp[sh][:, 0:Tb], k2tmp[sh][:, 0:Tb],
                                ALU.mult), waits=(t1,) + sl_k2[sh].write_waits())
                            sl_k2[sh].wrote(t2)
                            sl_kt[sh].used(t2)
                            tsp = b.emit_sdma(lambda e, off=off, Tb=Tb, dg=d_glob, sh=sh:
                                              e.dma_start(cm3(s_k2)[:, dg, off:off + Tb],
                                                          k2st[sh][:, 0:Tb]), waits=(t2,))
                            sl_k2[sh].used(tsp)

    def p6a():
        phase_open()
        xnres = A3("p6xn", BF16, NCH, 2048)
        g8 = [A83(f"p6g8_{h}", NCH, 512) for h in range(2)]
        u8 = [A83(f"p6u8_{h}", NCH, 512) for h in range(2)]
        gbf = [A3(f"p6g_{h}", BF16, NCH, 512) for h in range(2)]
        ubf = [A3(f"p6u_{h}", BF16, NCH, 512) for h in range(2)]
        gt = [A(f"p6gt{h}", F32, 512) for h in range(2)]
        ut = [A(f"p6ut{h}", F32, 512) for h in range(2)]
        fst = [A(f"p6fst{h}", BF16, 512) for h in range(2)]
        d8 = [A8(f"p6d8_{h}", 2048) for h in range(2)]
        dbf = [A(f"p6dbf{h}", BF16, 2048) for h in range(2)]

        sl_g8 = [Slot(), Slot()]; sl_u8 = [Slot(), Slot()]
        sl_g = [Slot(), Slot()]; sl_u = [Slot(), Slot()]
        sl_gt = [Slot(), Slot()]; sl_ut = [Slot(), Slot()]
        sl_f = [Slot(), Slot()]
        sl_d8 = [Slot(), Slot()]; sl_dbf = [Slot(), Slot()]

        txn = b.emit("sync", lambda e: e.dma_start(
            xnres, cm3(s_xnbf)[:, :, HALO:HALO + 2048]))
        gw3 = gw.rearrange("(a p) o -> p a o", p=128)
        uw3 = uw.rearrange("(a p) o -> p a o", p=128)
        dw3 = dw.rearrange("(a p) h -> p a h", p=128)

        dquant_i = [0]

        def emit_dquant():
            ich = dquant_i[0]
            if ich >= NICH:
                return
            dquant_i[0] += 1
            h = ich % 2
            t1 = b.emit("gpsimd", lambda e, ich=ich, h=h: e.dma_start(
                d8[h], dw3[:, ich, :]), waits=sl_d8[h].write_waits())
            sl_d8[h].wrote(t1)
            t2 = b.emit("gpsimd", lambda e, h=h: e.dma_start(dbf[h], d8[h]),
                        waits=(t1,) + sl_dbf[h].write_waits())
            sl_dbf[h].wrote(t2); sl_d8[h].used(t2)
            t3 = b.emit("sync", lambda e, ich=ich, h=h: e.dma_start(
                s_dwq.rearrange("a p t -> p a t")[:, ich, :], dbf[h]), waits=(t2,))
            sl_dbf[h].used(t3)

        for ib in range(12):
            h = ib % 2
            tg8 = b.emit("gpsimd", lambda e, ib=ib, h=h: e.dma_start(
                g8[h], gw3[:, :, ib * 512:(ib + 1) * 512]), waits=sl_g8[h].write_waits())
            sl_g8[h].wrote(tg8)
            tg = b.emit("gpsimd", lambda e, h=h: e.dma_start(gbf[h], g8[h]),
                        waits=(tg8,) + sl_g[h].write_waits())
            sl_g[h].wrote(tg); sl_g8[h].used(tg)
            tu8 = b.emit("gpsimd", lambda e, ib=ib, h=h: e.dma_start(
                u8[h], uw3[:, :, ib * 512:(ib + 1) * 512]), waits=sl_u8[h].write_waits())
            sl_u8[h].wrote(tu8)
            tu = b.emit("gpsimd", lambda e, h=h: e.dma_start(ubf[h], u8[h]),
                        waits=(tu8,) + sl_u[h].write_waits())
            sl_u[h].wrote(tu); sl_u8[h].used(tu)
            for _ in range(4):
                emit_dquant()
            for ti, (ot, Tb) in enumerate([(0, 512), (512, 512), (1024, 512), (1536, 512)]):
                for s in range(4):
                    q = ib * 4 + s
                    iG, pG, wG = pool.acquire()
                    iU, pU, wU = pool.acquire()
                    tokG = tokU = None
                    for j in range(NCH):
                        tokG = b.emit("tensor", lambda e, j=j, s=s, pG=pG, h=h, ot=ot, Tb=Tb:
                                      e.matmul(pG[:, 0:Tb],
                                               gbf[h][:, j, s * 128:(s + 1) * 128],
                                               xnres[:, j, ot:ot + Tb],
                                               start=(j == 0), stop=(j == NCH - 1)),
                                      waits=(wG + (tg, txn)) if j == 0 else (),
                                      inc=(1 if j == NCH - 1 else 0))
                    for j in range(NCH):
                        tokU = b.emit("tensor", lambda e, j=j, s=s, pU=pU, h=h, ot=ot, Tb=Tb:
                                      e.matmul(pU[:, 0:Tb],
                                               ubf[h][:, j, s * 128:(s + 1) * 128],
                                               xnres[:, j, ot:ot + Tb],
                                               start=(j == 0), stop=(j == NCH - 1)),
                                      waits=(wU + (tu, txn)) if j == 0 else (),
                                      inc=(1 if j == NCH - 1 else 0))
                    sl_g[h].used(tokG); sl_u[h].used(tokU)
                    sh = (ti * 4 + s) % 2
                    t1 = b.emit("scalar", lambda e, pG=pG, Tb=Tb, sh=sh, q=q: e.activation(
                        gt[sh][:, 0:Tb], pG[:, 0:Tb], AF.Silu, scale=c_gs[:, q:q + 1]),
                        waits=(tokG,) + sl_gt[sh].write_waits())
                    sl_gt[sh].wrote(t1)
                    pool.release(iG, t1)
                    t2 = b.emit("scalar", lambda e, pU=pU, Tb=Tb, sh=sh, q=q: e.mul(
                        ut[sh][:, 0:Tb], pU[:, 0:Tb], c_us[:, q:q + 1]),
                        waits=(tokU,) + sl_ut[sh].write_waits())
                    sl_ut[sh].wrote(t2)
                    pool.release(iU, t2)
                    t3 = b.emit("vector", lambda e, Tb=Tb, sh=sh: e.tensor_tensor(
                        fst[sh][:, 0:Tb], gt[sh][:, 0:Tb], ut[sh][:, 0:Tb], ALU.mult),
                        waits=(t1, t2) + sl_f[sh].write_waits())
                    sl_f[sh].wrote(t3)
                    sl_gt[sh].used(t3); sl_ut[sh].used(t3)
                    tsp = b.emit("sync", lambda e, q=q, ot=ot, Tb=Tb, sh=sh: e.dma_start(
                        cm3(s_fuse)[:, q, ot:ot + Tb], fst[sh][:, 0:Tb]), waits=(t3,))
                    sl_f[sh].used(tsp)
        while dquant_i[0] < NICH:
            emit_dquant()

    def p6b():
        phase_open()
        fu = [A3(f"p6bfu{h}", BF16, NICH, 512) for h in range(2)]
        dv = [A3(f"p6bdv{h}", BF16, NICH, 256) for h in range(2)]
        fst = [A(f"p6bfst{h}", F32, 512) for h in range(2)]
        xp = [A(f"p6bxp{h}", F32, 512) for h in range(2)]
        x2p = [A(f"p6bx2p{h}", F32, 512) for h in range(2)]
        sl_fu = [Slot(), Slot()]
        sl_dv = [Slot(), Slot()]
        sl_o = [Slot(), Slot()]
        sl_xp = [Slot(), Slot()]
        sl_x2p = [Slot(), Slot()]
        for ti, (ot, Tb) in enumerate([(0, 512), (512, 512), (1024, 512), (1536, 512)]):
            fh = ti % 2
            tf = b.emit("sync", lambda e, ot=ot, Tb=Tb, fh=fh: e.dma_start(
                fu[fh][:, :, 0:Tb], cm3(s_fuse)[:, :, ot:ot + Tb]),
                waits=sl_fu[fh].write_waits())
            sl_fu[fh].wrote(tf)
            for hb in range(8):
                dh = (ti * 8 + hb) % 2
                td = b.emit("sync", lambda e, hb=hb, dh=dh: e.dma_start(
                    dv[dh], cm3(s_dwq)[:, :, hb * 256:(hb + 1) * 256]),
                    waits=sl_dv[dh].write_waits())
                sl_dv[dh].wrote(td)
                for hc in range(2):
                    d_glob = hb * 2 + hc
                    sh = (ti * 16 + hb * 2 + hc) % 2
                    txp = b.emit("sync", lambda e, dg=d_glob, ot=ot, Tb=Tb, sh=sh: e.dma_start(
                        xp[sh][:, 0:Tb], xT3[:, dg, ot + HALO:ot + HALO + Tb]),
                        waits=sl_xp[sh].write_waits())
                    sl_xp[sh].wrote(txp)
                    tx2 = b.emit("sync", lambda e, dg=d_glob, ot=ot, Tb=Tb, sh=sh: e.dma_start(
                        x2p[sh][:, 0:Tb], cm3(s_x2)[:, dg, ot + HALO:ot + HALO + Tb]),
                        waits=sl_x2p[sh].write_waits())
                    sl_x2p[sh].wrote(tx2)
                    ii, pp, wfree = pool.acquire()
                    tok = None
                    for ich in range(NICH):
                        tok = b.emit("tensor", lambda e, ich=ich, hc=hc, pp=pp, fh=fh, dh=dh,
                                     Tb=Tb: e.matmul(
                                         pp[:, 0:Tb], dv[dh][:, ich, hc * 128:(hc + 1) * 128],
                                         fu[fh][:, ich, 0:Tb],
                                         start=(ich == 0), stop=(ich == NICH - 1)),
                                     waits=(wfree + (tf, td)) if ich == 0 else (),
                                     inc=(1 if ich == NICH - 1 else 0))
                    sl_fu[fh].used(tok); sl_dv[dh].used(tok)
                    t1 = b.emit("scalar", lambda e, pp=pp, Tb=Tb, sh=sh, dg=d_glob: e.mul(
                        fst[sh][:, 0:Tb], pp[:, 0:Tb], cons["ds"][:, dg:dg + 1]),
                        waits=(tok,) + sl_o[sh].write_waits())
                    pool.release(ii, t1)
                    t2 = b.emit("vector", lambda e, Tb=Tb, sh=sh: e.tensor_tensor(
                        fst[sh][:, 0:Tb], fst[sh][:, 0:Tb], xp[sh][:, 0:Tb], ALU.add),
                        waits=(t1, txp))
                    sl_xp[sh].used(t2)
                    t3 = b.emit("vector", lambda e, Tb=Tb, sh=sh: e.tensor_tensor(
                        fst[sh][:, 0:Tb], fst[sh][:, 0:Tb], x2p[sh][:, 0:Tb], ALU.add),
                        waits=(tx2,))
                    sl_x2p[sh].used(t3)
                    sl_o[sh].wrote(t3)
                    tsp = b.emit("gpsimd", lambda e, ot=ot, Tb=Tb, dg=d_glob, sh=sh: e.dma_start(
                        cm3(s_ffn)[:, dg, ot:ot + Tb], fst[sh][:, 0:Tb]), waits=(t3,))
                    sl_o[sh].used(tsp)

    def p5():
        phase_open()
        HT = 256
        cwv = A3("p5cwv", BF16, NCH, 2048)
        k2v = [A3("p5k2x0", BF16, NCH, HT), A3("p5k2x1", BF16, NCH, HT)]
        r2v = [A3(f"p5r2{h}", BF16, NCH, HT) for h in range(2)]
        prev = [A3(f"p5pre{h}", F32, NCH, HT) for h in range(2)]
        t1v = [A(f"p5t1{h}", F32, HT) for h in range(2)]
        ov = [A(f"p5ov{h}", F32, HT) for h in range(2)]
        sl_k2 = [Slot(), Slot()]
        sl_big = {n: [Slot(), Slot()] for n in ("r2", "pre")}
        sl_t1 = [Slot(), Slot()]
        sl_o = [Slot(), Slot()]

        twq = []
        for qq in range(4):
            t = b.emit("gpsimd", lambda e, qq=qq: e.dma_start(
                cwv[:, :, qq * 512:(qq + 1) * 512],
                wT["cwvT"].rearrange("(a p) d -> p a d", p=128)[:, :, qq * 512:(qq + 1) * 512]))
            twq.append(t)

        def sub_loads(sb):
            off = OBLK[sb // 2][0] + (sb % 2) * HT
            ot = off - HALO
            sh = sb % 2
            toks = {}
            tk = b.emit("sync", lambda e, off=off, sh=sh: e.dma_start(
                k2v[sh], cm3(s_k2)[:, :, off:off + HT]), waits=sl_k2[sh].write_waits())
            sl_k2[sh].wrote(tk)
            toks["k2"] = tk
            for n, dstl, srcap, o_ in (("r2", r2v, s_r2, off), ("pre", prev, s_ffn, ot)):
                t = b.emit("sync", lambda e, dst=dstl[sh], s2=srcap, o2=o_: e.dma_start(
                    dst, cm3(s2)[:, :, o2:o2 + HT]),
                    waits=sl_big[n][sh].write_waits())
                sl_big[n][sh].wrote(t)
                toks[n] = t
            return toks

        toks_cur = sub_loads(0)
        for sb in range(8):
            off = OBLK[sb // 2][0] + (sb % 2) * HT
            ot = off - HALO
            sh = sb % 2
            kh = sb % 2
            toks = toks_cur
            if sb + 1 < 8:
                toks_cur = sub_loads(sb + 1)

            for dj in range(NCH):
                ph = dj % 2
                ii, pp, wfree = pool.acquire()
                tok = None
                for j in range(NCH):
                    tok = b.emit("tensor", lambda e, j=j, dj=dj, pp=pp, kh=kh: e.matmul(
                        pp[:, 0:HT], cwv[:, j, dj * 128:(dj + 1) * 128],
                        k2v[kh][:, j, 0:HT], start=(j == 0), stop=(j == NCH - 1)),
                        waits=(wfree + (toks["k2"], twq[dj // 4])) if j == 0 else (),
                        inc=(1 if j == NCH - 1 else 0))
                sl_k2[kh].used(tok)
                tt1 = b.emit("vector", lambda e, pp=pp, sh=sh, ph=ph, dj=dj: e.tensor_tensor(
                    t1v[ph], r2v[sh][:, dj, 0:HT], pp[:, 0:HT], ALU.mult),
                    waits=(tok, toks["r2"]) + sl_t1[ph].write_waits())
                sl_t1[ph].wrote(tt1)
                sl_big["r2"][sh].used(tt1)
                pool.release(ii, tt1)
                if debug:
                    tdbg = b.emit("sync", lambda e, dj=dj, ot=ot, ph=ph: e.dma_start(
                        cm3(s_t1)[:, dj, ot:ot + HT], t1v[ph]), waits=(tt1,))
                    sl_t1[ph].used(tdbg)
                to3 = b.emit("vector", lambda e, sh=sh, ph=ph, dj=dj: e.tensor_tensor(
                    ov[ph], prev[sh][:, dj, 0:HT], t1v[ph], ALU.add),
                    waits=(toks["pre"],) + sl_o[ph].write_waits())
                sl_big["pre"][sh].used(to3)
                sl_o[ph].wrote(to3)
                sl_t1[ph].used(to3)
                tsp = b.emit("gpsimd", lambda e, dj=dj, ot=ot, ph=ph: e.dma_start(
                    out3[:, dj, ot:ot + HT], ov[ph]), waits=(to3,))
                sl_o[ph].used(tsp)

    phase_fns = {"p1": p1, "p2": p2, "p3": p3, "p4": p4,
                 "p6a": p6a, "p6b": p6b, "p5": p5}
    for ph in phases:
        b.phase = ph
        phase_fns[ph]()
        b.fence()
    phase_open()
    b.fence()

    sems = {sn: nc.alloc_semaphore(name="sm_" + sn) for sn in b.semnames}
    with nc.Block() as block:
        b.realize(nc, block, sems)
    return nc


def _prep_inputs(inputs):
    f32 = np.float32
    x = np.ascontiguousarray(np.asarray(inputs["L0_attn_x"], dtype=f32)).reshape(16384, D)

    def t(w):
        return np.ascontiguousarray(np.asarray(w, dtype=f32).T)

    shared = {
        "wrT": t(inputs["L0_attn_tm_Wr"]), "wkT": t(inputs["L0_attn_tm_Wk"]),
        "wvT": t(inputs["L0_attn_tm_Wv"]), "woT": t(inputs["L0_attn_tm_Wout"]),
        "cwrT": t(inputs["L0_attn_cm_Wr"]), "cwkT": t(inputs["L0_attn_cm_Wk"]),
        "cwvT": t(inputs["L0_attn_cm_Wv"]),
        "gw": np.ascontiguousarray(np.asarray(inputs["L0_ffn_gate_w"], dtype=f32)),
        "uw": np.ascontiguousarray(np.asarray(inputs["L0_ffn_up_w"], dtype=f32)),
        "dw": np.ascontiguousarray(np.asarray(inputs["L0_ffn_down_w"], dtype=f32)),
    }

    def v16(name):
        return np.ascontiguousarray(np.asarray(inputs[name], dtype=f32).reshape(NCH, 128).T)

    shared.update({
        "i_g1": v16("L0_attn_ln1_g"), "i_b1": v16("L0_attn_ln1_b"),
        "i_g2": v16("L0_attn_ln2_g"), "i_b2": v16("L0_attn_ln2_b"),
        "i_mur": v16("L0_attn_tm_mu_r"), "i_muk": v16("L0_attn_tm_mu_k"),
        "i_muv": v16("L0_attn_tm_mu_v"), "i_uu": v16("L0_attn_tm_u"),
        "i_cmur": v16("L0_attn_cm_mu_r"), "i_cmuk": v16("L0_attn_cm_mu_k"),
        "i_ds": v16("L0_ffn_down_scale"),
        "i_gs": np.ascontiguousarray(
            np.asarray(inputs["L0_ffn_gate_scale"], dtype=f32).reshape(NICH, 128).T),
        "i_us": np.ascontiguousarray(
            np.asarray(inputs["L0_ffn_up_scale"], dtype=f32).reshape(NICH, 128).T),
        "i_onescol": np.ones((128, 1), dtype=ml_dtypes.bfloat16),
        "i_onesrow": np.ones((1, 128), dtype=f32),
        "i_eps": np.full((128, 1), EPS, dtype=f32),
    })

    in_maps = []
    for c in range(8):
        start = c * NOUT
        if c % 4 == 0:
            halo = x[start:start + HALO]
        else:
            halo = x[start - HALO:start]
        xs = np.concatenate([halo, x[start:start + NOUT]], axis=0)
        m = np.ones((128, 8), dtype=f32)
        m[:, 0] = 0.0
        if c % 4 == 0:
            m[:, 1] = 0.0
        im = dict(shared)
        im["xT"] = np.ascontiguousarray(xs.T)
        im["i_mask"] = m
        in_maps.append(im)
    return in_maps


def kernel(**inputs):
    in_maps = _prep_inputs(inputs)
    nc = build_nc()
    res = run_bass_kernel_spmd(nc, in_maps, core_ids=list(range(8)))
    shards = [np.ascontiguousarray(np.asarray(r["out"], dtype=np.float32)).T
              for r in res.results]
    return np.concatenate(shards, axis=0).reshape(2, 8192, D).astype(np.float32)
